# version 36 (final)
# baseline (speedup 1.0000x reference)
import numpy as np
import ml_dtypes
import concourse.bass as bass
import concourse.mybir as mybir
from concourse.bass_utils import run_bass_kernel_spmd

F32 = mybir.dt.float32
BF16 = mybir.dt.bfloat16
I32 = mybir.dt.int32
AF = mybir.ActivationFunctionType
ALU = mybir.AluOpType
AX = mybir.AxisListType

MAGIC = float(np.float32(12582912.0))
TWO_PI = float(np.float32(2 * np.pi))
INV_2PI = float(np.float32(1.0 / (2 * np.pi)))
PI_CLAMP = 3.14159203
HALF_PI = float(np.float32(np.pi / 2))
EXP_SHIFT = 16.0
LN_EPS = 1e-5

CFG_FULL = dict(D=4096, S=4096, H=32, NR=15, E=8, R=16)



class _Eng:
    def __init__(self, rec, name, chain_sem, chained):
        self._rec, self._name = rec, name
        self._chain, self._chained = chain_sem, chained
        self._n = 0
        self._waited = {}
        self.items = []

    def wait(self, token):
        if token is None:
            return
        sem, val = token
        key = id(sem)
        if self._waited.get(key, -1) >= val:
            return
        self._waited[key] = val
        self.items.append(("wait", sem, val))

    def op(self, meth, *args, waits=(), produce=False, inc=None, **kw):
        for tk in waits:
            self.wait(tk)
        if self._chained and self._n:
            self.wait((self._chain, self._n))
        tok, incs = None, None
        if inc is not None:
            sem, by = inc
            c = self._rec.sem_counts
            c[id(sem)] = c.get(id(sem), 0) + by
            tok, incs = (sem, c[id(sem)]), (sem, by)
        elif self._chained or produce:
            self._n += 1
            tok, incs = (self._chain, self._n), (self._chain, 1)
        self.items.append(("call", meth, args, kw, incs))
        return tok

    def dma(self, out, in_, sem, waits=(), **kw):
        h = self._rec.dsem(sem)
        return self.op("dma_start", waits=waits, inc=(h, 16), out=out, in_=in_, **kw)

    def dma_tr(self, out, in_, sem, waits=()):
        h = self._rec.dsem(sem)
        return self.op("dma_start_transpose", out, in_, waits=waits, inc=(h, 16))


class Rec:
    def __init__(self, nc):
        self.nc = nc
        self.sem_counts = {}
        self.dsems = {}
        self.t = _Eng(self, "tensor", nc.alloc_semaphore("ch_pe"), chained=False)
        self.v = _Eng(self, "vector", nc.alloc_semaphore("ch_dve"), chained=True)
        self.a = _Eng(self, "scalar", nc.alloc_semaphore("ch_act"), chained=True)
        self.g = _Eng(self, "gpsimd", nc.alloc_semaphore("ch_pool"), chained=True)
        self.s = _Eng(self, "sync", nc.alloc_semaphore("ch_sp"), chained=False)
        self.engs = {"tensor": self.t, "vector": self.v, "scalar": self.a,
                     "gpsimd": self.g, "sync": self.s}

    def dsem(self, name):
        if name not in self.dsems:
            self.dsems[name] = self.nc.alloc_semaphore("dq_" + name)
        return self.dsems[name]

    def queue_totals(self):
        return [(h, self.sem_counts.get(id(h), 0)) for h in self.dsems.values()
                if self.sem_counts.get(id(h), 0) > 0]

    def barrier(self, tokens=()):
        alltk = list(tokens) + self.queue_totals() + \
                [(e._chain, e._n) for e in self.engs.values() if e._n > 0]
        for eng in self.engs.values():
            for tk in alltk:
                eng.wait(tk)

    def replay(self):
        with self.nc.Block() as block:
            for name, eng in self.engs.items():
                def make(items):
                    def run(e):
                        for it in items:
                            if it[0] == "wait":
                                e.wait_ge(it[1], it[2])
                            else:
                                _, meth, args, kw, incs = it
                                r = getattr(e, meth)(*args, **kw)
                                if incs is not None:
                                    r.then_inc(incs[0], incs[1])
                    return run
                getattr(block, name)(make(eng.items))



def build_program(cfg):
    D, S, H, NR, E, R = (cfg[k] for k in ("D", "S", "H", "NR", "E", "R"))
    HD = 128
    assert D == H * HD
    DC = D // 128
    TOWN = S // 2
    NT = TOWN // 128
    NTB = S // 128
    QTR = TOWN // 2
    NQT = QTR // 128
    assert QTR % 512 == 0, "psum zero regions are bank-granular"

    ER = E * R
    assert ER == 128
    VW = min(512, D)
    NVQ = D // VW
    QH = min(512, TOWN)
    NQH = TOWN // QH
    NSEG = (TOWN + 1023) // 1024
    OHALF = min(cfg.get('OHALF', 2048), D)
    NOH = D // OHALF
    ATT_SCALE = float(1.0 / np.sqrt(HD))

    nc = bass.Bass("TRN2", target_bir_lowering=False)

    x_in = nc.dram_tensor("x_perm", [S, D], F32, kind="ExternalInput").ap()
    pos_in = nc.dram_tensor("pos_perm", [1, S], F32, kind="ExternalInput").ap()
    mpart_in = nc.dram_tensor("m_part", [128, 128], BF16, kind="ExternalInput").ap()
    wq_in = nc.dram_tensor("wq", [D, D], F32, kind="ExternalInput").ap()
    wk_in = nc.dram_tensor("wk", [D, D], F32, kind="ExternalInput").ap()
    wv_in = nc.dram_tensor("wv", [D, D], F32, kind="ExternalInput").ap()
    wo_in = nc.dram_tensor("wo", [D, D], F32, kind="ExternalInput").ap()
    bW_in = nc.dram_tensor("base_W", [D, D], F32, kind="ExternalInput").ap()
    bb_in = nc.dram_tensor("base_b", [DC, 128], F32, kind="ExternalInput").ap()
    rW_in = nc.dram_tensor("router_W", [NR, D], F32, kind="ExternalInput").ap()
    rb_in = nc.dram_tensor("router_b", [1, NR], F32, kind="ExternalInput").ap()
    A_in = nc.dram_tensor("A_flat", [ER, D], F32, kind="ExternalInput").ap()
    B_in = nc.dram_tensor("B_w", [E * D, R], F32, kind="ExternalInput").ap()
    emap_in = nc.dram_tensor("expert_map", [NR, 1], I32, kind="ExternalInput").ap()
    out_d = nc.dram_tensor("out_own", [TOWN, D], F32, kind="ExternalOutput").ap()

    ident_bf_d = nc.inline_tensor(np.eye(128, dtype=ml_dtypes.bfloat16), "c_identb").ap()
    ident_f_d = nc.inline_tensor(np.eye(128, dtype=np.float32), "c_identf").ap()
    invf_np = (10000.0 ** (-(np.arange(128) % 64) * 2.0 / HD)).astype(np.float32)
    invf_d = nc.inline_tensor(invf_np.reshape(1, 128), "c_invf").ap()
    sgn_d = nc.inline_tensor(
        np.where(np.arange(128) < 64, 1.0, -1.0).astype(np.float32).reshape(128, 1),
        "c_sgn").ap()
    diag_d = nc.inline_tensor(
        (np.arange(128)[None, :] >= np.arange(128)[:, None]).astype(ml_dtypes.bfloat16),
        "c_diag").ap()
    exp16_d = nc.inline_tensor(
        (np.arange(ER)[None, :] // R == np.arange(E)[:, None]).astype(ml_dtypes.bfloat16),
        "c_exp16").ap()
    eiota_d = nc.inline_tensor(
        np.tile(np.arange(E, dtype=np.float32), (NR, 1)), "c_eiota").ap()

    wq_t = nc.dram_tensor("wq_t", [H, 128, DC, 128], BF16, kind="Internal")
    wk_t = nc.dram_tensor("wk_t", [H, 128, DC, 128], BF16, kind="Internal")
    wv_t = nc.dram_tensor("wv_t", [H, 128, DC, 128], BF16, kind="Internal")
    bW_b = nc.dram_tensor("bW_b", [D, D], BF16, kind="Internal")
    B_b = nc.dram_tensor("B_b", [E * D, R], BF16, kind="Internal")
    BT_park = nc.dram_tensor("BT_park", [ER, D], BF16, kind="Internal")
    K_park = nc.dram_tensor("K_park", [H, 128, S], BF16, kind="Internal")
    Q_park = nc.dram_tensor("Q_park", [H, 128, TOWN], BF16, kind="Internal")
    VT_park = nc.dram_tensor("VT_park", [H, 128, S], BF16, kind="Internal")
    FFN_park = nc.dram_tensor("FFN_park", [D, TOWN], BF16, kind="Internal")
    LORA_park = nc.dram_tensor("LORA_park", [D, TOWN], BF16, kind="Internal")
    A_db = nc.dram_tensor("A_db", [ER, D], BF16, kind="Internal")
    AO_park = nc.dram_tensor("AO_park", [H, 128, TOWN], BF16, kind="Internal")

    rec = Rec(nc)
    t, v, a, g, s = rec.t, rec.v, rec.a, rec.g, rec.s

    PSALL = nc.alloc_psum_tensor("psall", [128, 4096], F32)
    PS = [PSALL.ap()[:, i * 1024:(i + 1) * 1024] for i in range(4)]
    psA, psB, psC, psD = PS

    def sbt(name, shape, dt=F32):
        return nc.alloc_sbuf_tensor(name, shape, dt)
    ident_bs = sbt("ident_bs", [128, 128], BF16)
    ident_fs = sbt("ident_fs", [128, 128], F32)
    invf_s = sbt("invf_s", [1, 128], F32)
    sgn_s = sbt("sgn_s", [128, 1], F32)
    diag_s = sbt("diag_s", [128, 128], BF16)
    mpart_s = sbt("mpart_s", [128, 128], BF16)
    mp2 = sbt("mp2", [128, 2, 128], BF16)
    exp16_s = sbt("exp16_s", [E, 128], BF16)
    eio_s = sbt("eio_s", [NR, E], F32)
    emap_i = sbt("emap_i", [NR, 1], I32)
    emap_f = sbt("emap_f", [NR, 1], F32)
    sel_f = sbt("sel_f", [NR, E], F32)
    sel_b = sbt("sel_b", [NR, E], BF16)
    ones_col = sbt("ones_col", [128, 1], BF16)
    negc_col = sbt("negc_col", [128, 1], F32)
    ones_r128 = sbt("ones_r128", [1, 128], F32)
    rb_row = sbt("rb_row", [1, NR], F32)
    bb_dc = sbt("bb_dc", [DC, 128], F32)
    bb_cols = sbt("bb_cols", [128, DC], F32)
    stats = sbt("stats", [128, 8], F32)
    posq = sbt("posq", [1, max(QTR, 128)], F32)
    lgs = sbt("lgs", [128, NR], F32)
    cur_s = sbt("cur_s", [128, NR], F32)
    eq_s = sbt("eq_s", [128, NR], F32)
    pi_b = sbt("pi_b", [128, NR], BF16)
    lcol = sbt("lcol", [128, 6], F32)
    Bt_in = sbt("Bt_in", [128, 8, R], BF16)
    Bt_ev = sbt("Bt_ev", [R, 8 * 128], BF16)
    rw_pad = sbt("rw_pad", [16, D], BF16)
    invd_s = sbt("invd_s", [1, 1024], F32)

    AR_BYTES = 189696
    arena = nc.alloc_sbuf_tensor("arena", [128, AR_BYTES // 2], BF16)

    class Alloc:
        def __init__(self, off=0):
            self.off = off
        def take(self, shape, dt=BF16):
            esz = mybir.dt.size(dt)
            per_p = int(np.prod(shape[1:])) * esz
            o = (self.off + 31) // 32 * 32
            self.off = o + per_p
            assert self.off <= AR_BYTES, f"arena overflow {self.off}"
            flat = arena.ap().bitcast(dt)[0:shape[0], o // esz: (o + per_p) // esz]
            if len(shape) == 2:
                return flat
            dims = {f"d{i}": shape[i] for i in range(1, len(shape))}
            pat = ("p (" + " ".join(dims) + ") -> p " + " ".join(dims))
            return flat.rearrange(pat, **dims)

    al = Alloc()
    xnT = [al.take([128, DC, QTR]), al.take([128, DC, QTR])]
    P2_BASE = al.off
    wg_sb = al.take([128, 2, DC, 128])
    xbuf = al.take([128, D], F32)
    xn_tok = al.take([128, D])
    cosq = al.take([128, QTR])
    sinq = al.take([128, QTR])
    qk_e = al.take([128, 2, QTR])
    th_t = al.take([128, QTR], F32)
    rot_t1 = th_t.bitcast(BF16)[:, 0:QTR]

    al2 = Alloc(P2_BASE)
    bWT_t = al2.take([128, 2, DC, 128])
    AT_s = al2.take([128, DC, 128])
    rwT_s = al2.take([128, DC, 16])
    BT_s = AT_s.rearrange("p a b -> p (a b)")
    piT = al2.take([NR, TOWN])
    wT_s = al2.take([E, TOWN])
    wbc_s = al2.take([128, TOWN])
    gT_s = al2.take([128, TOWN])
    ffn_ev = al2.take([128, 2, 1024])
    lora_ev = al2.take([128, 2, 1024])
    bw_row = al2.take([128, D])

    al3 = Alloc()
    Kh_sb = al3.take([128, 2, S])
    Vh_sb = al3.take([128, 2, NTB, 128])
    Qh_sb = al3.take([128, 2, TOWN])
    pT_sb = al3.take([128, 8, QH])
    ao_sb = al3.take([128, 2, TOWN])
    dacc_g = al3.take([128, QH], F32)
    dacc_v = al3.take([128, QH], F32)
    ivb_s = al3.take([128, QH], F32)
    ffn_ld = al3.take([128, 2, TOWN])
    lora_ld = al3.take([128, 2, TOWN])

    al4 = Alloc()
    woh_sb = al4.take([128, DC, OHALF])
    atT_sb = al4.take([128, 2, DC, 128])
    ffn_p4 = al4.take([128, 2, OHALF])
    x_p4 = al4.take([128, 2, OHALF], F32)
    out_p4 = al4.take([128, 2, OHALF], F32)

    for dst, srcp in [(ident_bs[:], ident_bf_d), (ident_fs[:], ident_f_d),
                      (invf_s[:], invf_d), (sgn_s[:], sgn_d), (diag_s[:], diag_d),
                      (mpart_s[:], mpart_in), (exp16_s[:], exp16_d),
                      (eio_s[:], eiota_d), (emap_i[:], emap_in),
                      (rb_row[:], rb_in), (bb_dc[:], bb_in)]:
        tk_consts = s.dma(dst, srcp, "const")
    tk_idb = tk_idf = tk_invf = tk_sgn = tk_diag = tk_mp = tk_consts
    tk_x16 = tk_eio = tk_emap = tk_rb = tk_bb2 = tk_consts
    tk_mpa = v.op("tensor_copy", waits=(tk_diag,), out=mp2[:, 0, :], in_=diag_s[:])
    tk_mp2 = v.op("tensor_copy", waits=(tk_mp,), out=mp2[:, 1, :], in_=mpart_s[:])
    v.op("memset", ones_col[:], 1.0)
    tk_negC = v.op("memset", negc_col[:], -EXP_SHIFT)
    tk_ones = v.op("memset", ones_r128[:], 1.0)

    def cast_tiled(dst, src, sem):
        gtoks = []
        step = max(1, 8)
        for hg in range(0, H, 4):
            tk = None
            for h in range(hg, min(H, hg + 4)):
                for dq in range(0, DC, step):
                    dn = min(step, DC - dq)
                    tk = g.dma(dst.ap()[h, :, dq:dq + dn, :],
                               src.rearrange("(dc p) (h j) -> h p dc j",
                                             p=128, j=128)
                                  [h, :, dq:dq + dn, :],
                               "%s_%d" % (sem, hg // 4))
            gtoks.append(tk)
        return [gtoks[h // 4] for h in range(H)]
    tks_wk = cast_tiled(wk_t, wk_in, "g_wk")
    tks_wv = cast_tiled(wv_t, wv_in, "g_wv")
    tks_wq = cast_tiled(wq_t, wq_in, "g_wq")
    tk_bW_b = None
    for dc in range(DC):
        tk_bW_b = g.dma(bW_b.ap()[dc * 128:(dc + 1) * 128, :],
                        bW_in[dc * 128:(dc + 1) * 128, :], "g_bW")
    tk_B_b = g.dma(B_b.ap()[:, :], B_in, "g_B")

    tk_bbT = t.op("transpose", psA[0:128, 0:DC], produce=True,
                  waits=(tk_bb2, tk_idf),
                  in_=bb_dc[:], identity=ident_fs[0:DC, 0:DC])
    tk_bbc = v.op("tensor_copy", waits=(tk_bbT,), out=bb_cols[:],
                  in_=psA[0:128, 0:DC])

    v.op("tensor_copy", waits=(tk_emap,), out=emap_f[:], in_=emap_i[:])
    v.op("tensor_scalar", waits=(tk_eio,), out=sel_f[:], in0=eio_s[:],
         scalar1=emap_f[:], scalar2=None, op0=ALU.is_equal)
    tk_selb = v.op("tensor_copy", out=sel_b[:], in_=sel_f[:])

    last_xbuf = []
    last_xn = []
    last_stat = []
    last_xnT = [[], []]
    last_wg = [[], []]
    last_qk = [[], []]
    last_kq_ps = [[tk_bbc], []]
    last_tab = []
    last_thbuf = []
    last_rot = []
    last_posrd = []
    park_K, park_Q, park_V = [], [], []

    def ln_tile(qi, slot, tc):
        nonlocal last_xbuf, last_xn, last_stat
        buf = xnT[slot]
        row0 = qi * QTR + tc * 128
        tk_x = s.dma(xbuf[:], x_in[row0:row0 + 128, :], "s_x",
                     waits=tuple(last_xbuf))
        last_xbuf = []
        tk_sum = v.op("reduce_sum", waits=(tk_x,), out=stats[:, 0:1],
                      in_=xbuf[:], axis=AX.X)
        tk_sq = a.op("activation", waits=(tk_x,) + tuple(last_xn) + tuple(last_stat),
                     out=xn_tok[:],
                     in_=xbuf[:], func=AF.Square, accum_out=stats[:, 1:2])
        v.op("tensor_scalar", waits=(tk_sum,), out=stats[:, 2:3],
             in0=stats[:, 0:1], scalar1=float(1.0 / D), scalar2=None, op0=ALU.mult)
        v.op("tensor_tensor", out=stats[:, 3:4], in0=stats[:, 2:3],
             in1=stats[:, 2:3], op=ALU.mult)
        v.op("tensor_scalar", waits=(tk_sq,), out=stats[:, 4:5],
             in0=stats[:, 1:2], scalar1=float(1.0 / D), scalar2=None, op0=ALU.mult)
        v.op("tensor_tensor", out=stats[:, 4:5], in0=stats[:, 4:5],
             in1=stats[:, 3:4], op=ALU.subtract)
        tk_ve = v.op("tensor_scalar", out=stats[:, 5:6], in0=stats[:, 4:5],
                     scalar1=float(LN_EPS), scalar2=None, op0=ALU.add)
        last_stat = [tk_ve]
        a.op("activation", waits=(tk_ve,), out=stats[:, 6:7], in_=stats[:, 5:6],
             func=AF.Ln)
        tk_rs = a.op("activation", out=stats[:, 7:8], in_=stats[:, 6:7],
                     func=AF.Exp, scale=-0.5)
        tk_xn = v.op("tensor_scalar", waits=(tk_rs,), out=xn_tok[:], in0=xbuf[:],
                     scalar1=stats[:, 2:3], scalar2=stats[:, 7:8],
                     op0=ALU.subtract, op1=ALU.mult)
        last_xbuf = [tk_xn]
        tk_tr = a.dma_tr(buf[:, :, tc * 128:(tc + 1) * 128],
                         xn_tok.rearrange("p (c f) -> p c f", f=128),
                         "s_tr%d" % slot,
                         waits=tuple([tk_xn] + last_xnT[slot]))
        last_xnT[slot] = []
        last_xn = [tk_tr]
        return tk_tr

    def rotary_tables(qi):
        nonlocal last_tab, last_thbuf
        tk_p = s.dma(posq[:, 0:QTR], pos_in[:, qi * QTR:(qi + 1) * QTR], "s_pos",
                     waits=tuple(last_posrd))
        tk_th = None
        for off in range(0, QTR, 512):
            n = min(512, QTR - off)
            tk_th = t.op("matmul", psC[:, off:off + n],
                         waits=((tk_p, tk_invf) + tuple(last_tab)) if off == 0 else (),
                         produce=(off + n >= QTR),
                         lhsT=invf_s[:], rhs=posq[:, off:off + n],
                         start=True, stop=True)
        last_posrd[:] = [tk_th]
        last_tab = []
        sins = []
        for phase, dst in (("sin", sinq), ("cos", cosq)):
            shift = 0.0 if phase == "sin" else 0.25
            psh = 0.0 if phase == "sin" else HALF_PI
            v.op("tensor_scalar", waits=(tk_th,) + tuple(last_thbuf), out=th_t[:],
                 in0=psC[:, 0:QTR], scalar1=INV_2PI, scalar2=shift,
                 op0=ALU.mult, op1=ALU.add)
            last_thbuf = []
            v.op("tensor_scalar", out=th_t[:], in0=th_t[:],
                 scalar1=MAGIC, scalar2=MAGIC, op0=ALU.add, op1=ALU.subtract)
            v.op("tensor_scalar", out=th_t[:], in0=th_t[:],
                 scalar1=-TWO_PI, scalar2=None, op0=ALU.mult)
            tk2 = v.op("scalar_tensor_tensor", out=th_t[:], in0=psC[:, 0:QTR],
                       scalar=psh, in1=th_t[:], op0=ALU.add, op1=ALU.add)
            last_tab.append(tk2)
            tk_cl = v.op("tensor_scalar", out=th_t[:], in0=th_t[:],
                         scalar1=PI_CLAMP, scalar2=-PI_CLAMP, op0=ALU.min, op1=ALU.max)
            tk_sin = a.op("activation", out=dst[:, 0:QTR], in_=th_t[:], func=AF.Sin,
                          waits=(tk_cl,) + tuple(last_rot))
            sins.append(tk_sin)
            last_thbuf.append(tk_sin)
        tk_fold = v.op("tensor_scalar", waits=(sins[0], tk_sgn), out=sinq[:, 0:QTR],
                       in0=sinq[:, 0:QTR], scalar1=sgn_s[:], scalar2=None,
                       op0=ALU.mult)
        return (sins[1], tk_fold)

    def rotary_apply(sl, tk_src, tk_tab):
        w = (tk_src,) + tk_tab
        v.op("tensor_tensor", waits=w, out=rot_t1[0:64, 0:QTR],
             in0=sl[64:128, :], in1=sinq[64:128, 0:QTR], op=ALU.mult)
        v.op("tensor_tensor", out=rot_t1[64:128, 0:QTR],
             in0=sl[0:64, :], in1=sinq[0:64, 0:QTR], op=ALU.mult)
        v.op("tensor_tensor", out=sl, in0=sl, in1=cosq[:, 0:QTR], op=ALU.mult)
        tk = v.op("tensor_tensor", out=sl, in0=sl, in1=rot_t1[:, 0:QTR], op=ALU.add)
        last_rot[:] = [tk]
        return tk

    QORDER = [2, 3, 0, 1]
    ln_toks = {QORDER[0]: [ln_tile(QORDER[0], 0, tc) for tc in range(NQT)]}
    for oi, qi in enumerate(QORDER):
        slot = oi % 2
        tr_toks = ln_toks[qi]
        buf = xnT[slot]
        tk_tab = rotary_tables(qi)
        own = qi < 2
        nxt = QORDER[oi + 1] if oi + 1 < len(QORDER) else None
        ln_next = []
        nheads_tot = H * (2 if own else 1)
        ln_every = max(1, nheads_tot // NQT)
        hcount = 0

        def maybe_ln():
            nonlocal hcount
            hcount += 1
            if nxt is not None and hcount % ln_every == 0 and len(ln_next) < NQT:
                ln_next.append(ln_tile(nxt, (oi + 1) % 2, len(ln_next)))

        projs = [("K", wk_t, tks_wk, K_park, qi * QTR),
                 ("V", wv_t, tks_wv, VT_park, qi * QTR)]
        if own:
            projs.append(("Q", wq_t, tks_wq, Q_park, qi * QTR))
        for pname, w_t, wtoks, park, col0 in projs:
            pk_list = {"K": park_K, "Q": park_Q, "V": park_V}[pname]
            for h in range(H):
                pslot = h % 2
                pbuf = psA if pslot == 0 else psB
                tk_w = s.dma(wg_sb[:, pslot, :, :],
                             w_t.ap()[h, :, :, :], "s_wg%d" % pslot,
                             waits=tuple(last_wg[pslot]) + (wtoks[h],))
                last_wg[pslot] = []
                tkmm = None
                for dc in range(DC):
                    for off in range(0, QTR, 512):
                        n = min(512, QTR - off)
                        tkmm = t.op(
                            "matmul", pbuf[:, off:off + n],
                            waits=((tk_w, tr_toks[-1]) + tuple(last_kq_ps[pslot]))
                                  if (dc == 0 and off == 0) else (),
                            produce=(dc == DC - 1 and off + n >= QTR),
                            lhsT=wg_sb[:, pslot, dc, :],
                            rhs=buf[:, dc, off:off + n],
                            start=(dc == 0), stop=(dc == DC - 1))
                last_kq_ps[pslot] = []
                last_wg[pslot].append(tkmm)
                sl = qk_e[:, pslot, :]
                tk_ev = a.op("activation", waits=(tkmm,) + tuple(last_qk[pslot]),
                             out=sl, in_=pbuf[:, 0:QTR], func=AF.Copy)
                last_kq_ps[pslot].append(tk_ev)
                tk_ro = tk_ev if pname == "V" else rotary_apply(sl, tk_ev, tk_tab)
                tk_pk = a.dma(park.ap()[h, :, col0:col0 + QTR], sl,
                              "a_pk%d" % pslot, waits=(tk_ro,))
                last_qk[pslot] = [tk_pk]
                pk_list.append(tk_pk)
                maybe_ln()
        while nxt is not None and len(ln_next) < NQT:
            ln_next.append(ln_tile(nxt, (oi + 1) % 2, len(ln_next)))
        if nxt is not None:
            ln_toks[nxt] = ln_next

        last_xnT[slot] = [tkmm]

    rec.barrier()

    tk_A_db = g.dma(A_db.ap(), A_in, "g_A")
    tk_rwz = v.op("memset", rw_pad[:], 0.0)
    tk_rwl = g.dma(rw_pad[0:NR, :], rW_in, "g_rw", waits=(tk_rwz,))
    tk_rwT = s.dma_tr(rwT_s[:], rw_pad.ap().rearrange("p (c f) -> p c f", f=128),
                      "s_p2tr", waits=(tk_rwl,))
    tk_AT = s.dma_tr(AT_s[:], A_db.ap().rearrange("p (c f) -> p c f", f=128),
                     "s_p2at", waits=(tk_A_db,))

    last_r_ps = [[], []]
    Z = {"tk_pe": None, "tk_pi": None, "tk_pt_prev": None,
         "tk_BT": None, "tk_btr_prev": None, "tk_bev_prev": None,
         "tk_wT": None, "tk_wTe": None, "tk_wbc": None, "tk_wbce": None,
         "tk_hT": None, "tk_gT": None, "tk_BTs": None,
         "last_lo": [[], []], "last_lev": [[], []]}

    def mk_tc(tc):
        def u():
            pslot = tc % 2
            pbuf = PS[2] if pslot == 0 else PS[3]
            for dc in range(DC):
                bufq = xnT[(tc * 128) // QTR]
                t.op("matmul", pbuf[:, 0:NR],
                     waits=((tk_rwT,) + tuple(last_r_ps[pslot])) if dc == 0 else (),
                     lhsT=bufq[:, dc, (tc * 128) % QTR:(tc * 128) % QTR + 128],
                     rhs=rwT_s[:, dc, 0:NR],
                     start=(dc == 0), stop=False)
            last_r_ps[pslot][:] = []
            tkmm = t.op("matmul", pbuf[:, 0:NR], produce=True,
                        waits=(tk_rb, tk_ones),
                        lhsT=ones_r128[:], rhs=rb_row[:], start=False, stop=True)
            tk_c = v.op("tensor_copy", waits=(tkmm,), out=lgs[:], in_=pbuf[:, 0:NR])
            last_r_ps[pslot].append(tk_c)
            v.op("reduce_max", out=lcol[:, 0:1], in_=lgs[:], axis=AX.X)
            tk_nm = v.op("tensor_scalar", out=lcol[:, 1:2], in0=lcol[:, 0:1],
                         scalar1=-1.0, scalar2=None, op0=ALU.mult)
            tk_e = a.op("activation", waits=(tk_nm,), out=lgs[:], in_=lgs[:],
                        func=AF.Exp, bias=lcol[:, 1:2], accum_out=lcol[:, 2:3])
            v.op("reciprocal", waits=(tk_e,), out=lcol[:, 3:4], in_=lcol[:, 2:3])
            v.op("tensor_scalar", out=lgs[:], in0=lgs[:],
                 scalar1=lcol[:, 3:4], scalar2=None, op0=ALU.mult)
            v.op("tensor_copy", out=cur_s[:], in_=lgs[:])
            for _ in range(E - 1):
                v.op("reduce_max", out=lcol[:, 4:5], in_=cur_s[:], axis=AX.X)
                v.op("tensor_scalar", out=eq_s[:], in0=cur_s[:],
                     scalar1=lcol[:, 4:5], scalar2=None, op0=ALU.is_equal)
                v.op("scalar_tensor_tensor", out=cur_s[:], in0=eq_s[:],
                     scalar=-2.0, in1=cur_s[:], op0=ALU.mult, op1=ALU.add)
            v.op("reduce_max", out=lcol[:, 5:6], in_=cur_s[:], axis=AX.X)
            v.op("tensor_scalar", out=eq_s[:], in0=lgs[:],
                 scalar1=lcol[:, 5:6], scalar2=None, op0=ALU.is_ge)
            tk_pi = v.op("tensor_tensor",
                         waits=(Z["tk_pt_prev"],) if Z["tk_pt_prev"] else (),
                         out=pi_b[:], in0=lgs[:], in1=eq_s[:],
                         op=ALU.mult)
            Z["tk_pi"] = tk_pi
        return u

    def mk_tc_b(tc):
        def u():
            pslot = tc % 2
            pbuf = PS[2] if pslot == 0 else PS[3]
            tk_pt = t.op("transpose", pbuf.bitcast(BF16)[0:NR, 1024:1024 + 128],
                         produce=True, waits=(Z["tk_pi"], tk_idb),
                         in_=pi_b[:], identity=ident_bs[:])
            Z["tk_pt_prev"] = tk_pt
            tk_pe = a.op("activation", waits=(tk_pt,),
                         out=piT[:, tc * 128:(tc + 1) * 128],
                         in_=pbuf.bitcast(BF16)[0:NR, 1024:1024 + 128],
                         func=AF.Copy)
            Z["tk_pe"] = tk_pe
            last_r_ps[pslot].append(tk_pe)
        return u

    def mk_bt(e, oc0):
        def u():
            nb = min(8, DC - oc0)
            tk_bl = g.dma(Bt_in[:, 0:nb, :],
                          B_b.ap()[e * D + oc0 * 128: e * D + (oc0 + nb) * 128, :]
                             .rearrange("(c p) r -> p c r", p=128),
                          "g_bt",
                          waits=(tk_B_b,) +
                                ((Z["tk_btr_prev"],) if Z["tk_btr_prev"] else ()))
            tk_btr = None
            for i in range(nb):
                tk_btr = t.op("transpose",
                              PS[2].bitcast(BF16)[0:R, 1024 + i * 128:1152 + i * 128],
                              produce=(i == nb - 1),
                              waits=((tk_bl, tk_idb) + tuple(last_r_ps[0]) +
                                     ((Z["tk_bev_prev"],) if Z["tk_bev_prev"] else ()))
                                    if i == 0 else (),
                              in_=Bt_in[:, i, :], identity=ident_bs[:])
            Z["tk_btr_prev"] = tk_btr
            last_r_ps[0][:] = []
            tk_bev = a.op("activation",
                          waits=(tk_btr,) + ((Z["tk_BT"],) if Z["tk_BT"] else ()),
                          out=Bt_ev[:, 0:nb * 128],
                          in_=PS[2].bitcast(BF16)[0:R, 1024:1024 + nb * 128],
                          func=AF.Copy, scale=2.0)
            Z["tk_bev_prev"] = tk_bev
            last_r_ps[0].append(tk_bev)
            Z["tk_BT"] = g.dma(BT_park.ap()[e * R:(e + 1) * R,
                                            oc0 * 128:(oc0 + nb) * 128],
                               Bt_ev[:, 0:nb * 128], "g_btp", waits=(tk_bev,))
        return u

    def u_wT():
        tk = None
        w0 = (tk_selb,) + tuple(last_r_ps[0]) + tuple(last_r_ps[1]) + \
             ((Z["tk_bev_prev"],) if Z["tk_bev_prev"] else ()) + \
             ((Z["tk_pe"],) if Z["tk_pe"] else ())
        for off in range(0, TOWN, 512):
            n = min(512, TOWN - off)
            tk = t.op("matmul", PS[2 + off // 1024][0:E, off % 1024: off % 1024 + n],
                      waits=w0 if off == 0 else (),
                      produce=(off + n >= TOWN),
                      lhsT=sel_b[:], rhs=piT[:, off:off + n], start=True, stop=True)
        Z["tk_wT"] = tk
        tke = None
        for i in range(NSEG):
            n = min(1024, TOWN - i * 1024)
            tke = a.op("activation", waits=(Z["tk_wT"],),
                       out=wT_s[:, i * 1024:i * 1024 + n],
                       in_=PS[2 + i][0:E, 0:n], func=AF.Copy)
        Z["tk_wTe"] = tke

    def u_wbc():
        tk = None
        for off in range(0, TOWN, 512):
            n = min(512, TOWN - off)
            tk = t.op("matmul", PS[2 + off // 1024][:, off % 1024: off % 1024 + n],
                      waits=(tk_x16, Z["tk_wTe"]) if off == 0 else (),
                      produce=(off + n >= TOWN),
                      lhsT=exp16_s[:], rhs=wT_s[:, off:off + n], start=True, stop=True)
        Z["tk_wbc"] = tk
        tke = None
        for i in range(NSEG):
            n = min(1024, TOWN - i * 1024)
            tke = a.op("activation", waits=(Z["tk_wbc"],),
                       out=wbc_s[:, i * 1024:i * 1024 + n],
                       in_=PS[2 + i][:, 0:n], func=AF.Copy)
        Z["tk_wbce"] = tke

    def mk_hT(dc0, dc1):
        def u():
            tk = Z["tk_hT"]
            for dc in range(dc0, dc1):
                c = 0
                while c < TOWN:
                    n = min(TOWN - c, 512 - (c % 512), QTR - (c % QTR))
                    tk = t.op("matmul", PS[2 + c // 1024][:, c % 1024: c % 1024 + n],
                              waits=(tk_AT, Z["tk_wbce"])
                                    if (dc == 0 and c == 0) else (),
                              produce=(dc == DC - 1 and c + n >= TOWN),
                              lhsT=AT_s[:, dc, :],
                              rhs=xnT[c // QTR][:, dc, c % QTR: c % QTR + n],
                              start=(dc == 0), stop=(dc == DC - 1))
                    c += n
            Z["tk_hT"] = tk
        return u

    def u_gT():
        tk = None
        for i in range(NSEG):
            n = min(1024, TOWN - i * 1024)
            tk = v.op("tensor_tensor", waits=(Z["tk_hT"], Z["tk_wbce"]),
                      out=gT_s[:, i * 1024:i * 1024 + n],
                      in0=PS[2 + i][:, 0:n],
                      in1=wbc_s[:, i * 1024:i * 1024 + n], op=ALU.mult)
            Z["last_lo"][i].append(tk)
        Z["tk_gT"] = tk
        Z["tk_BTs"] = g.dma(BT_s[:, 0:D], BT_park.ap()[:, :], "g_bts",
                            waits=(Z["tk_BT"], Z["tk_hT"]))

    def mk_lora(o8):
        def u():
            for oc in range(o8 * 8, o8 * 8 + 8):
                lpar = oc % 2
                tk_le = None
                for seg in range(NSEG):
                    pq = PS[2 + seg]
                    tk2 = None
                    for c2 in range(2):
                        tk2 = t.op("matmul", pq[:, c2 * 512:c2 * 512 + 512],
                                   waits=((Z["tk_BTs"], Z["tk_gT"]) +
                                          tuple(Z["last_lo"][seg]))
                                         if c2 == 0 else (),
                                   produce=(c2 == 1),
                                   lhsT=BT_s[:, oc * 128:(oc + 1) * 128],
                                   rhs=gT_s[:, seg * 1024 + c2 * 512:
                                            seg * 1024 + c2 * 512 + 512],
                                   start=True, stop=True)
                    Z["last_lo"][seg][:] = []
                    tk_le = a.op("activation",
                                 waits=(tk2,) + tuple(Z["last_lev"][lpar]),
                                 out=lora_ev[:, lpar, :],
                                 in_=pq[:, 0:1024], func=AF.Copy)
                    Z["last_lo"][seg].append(tk_le)
                    Z["last_lev"][lpar][:] = []
                    tk_lp = a.dma(LORA_park.ap()[oc * 128:(oc + 1) * 128,
                                                 seg * 1024:seg * 1024 + 1024],
                                  lora_ev[:, lpar, :], "a_lp%d" % lpar,
                                  waits=(tk_le,))
                    Z["last_lev"][lpar].append(tk_lp)
        return u

    Zb = {"last_bwt": [[], []], "last_f_ps": [[], []], "last_fev": [[], []],
          "tk_bt": [None, None], "prev_bt": None}

    def mk_base(oc, seg):
        def u():
            par = oc % 2
            if seg == 0:
                tk_bw = s.dma(bw_row[:], bW_b.ap()[oc * 128:(oc + 1) * 128, :],
                              "s_bw%d" % par,
                              waits=tuple(Zb["last_bwt"][par]) + (tk_bW_b,) +
                                    ((Zb["prev_bt"],) if Zb["prev_bt"] else ()))
                tk_bt = s.dma_tr(bWT_t[:, par, :, :],
                                 bw_row.rearrange("p (c f) -> p c f", f=128),
                                 "s_bw%d" % par, waits=(tk_bw,))
                Zb["prev_bt"] = tk_bt
                Zb["tk_bt"][par] = tk_bt
                Zb["last_bwt"][par][:] = []
            q = (2 * oc + seg) % 2
            tkmm = None
            for dc in range(DC):
                for c2 in range(2):
                    c = seg * 1024 + c2 * 512
                    tkmm = t.op("matmul", PS[q][:, c2 * 512:c2 * 512 + 512],
                                waits=((Zb["tk_bt"][par],) +
                                       tuple(Zb["last_f_ps"][q]))
                                      if (dc == 0 and c2 == 0) else (),
                                produce=(dc == DC - 1 and c2 == 1),
                                lhsT=bWT_t[:, par, dc, :],
                                rhs=xnT[c // QTR][:, dc, c % QTR: c % QTR + 512],
                                start=(dc == 0), stop=(dc == DC - 1))
            Zb["last_f_ps"][q][:] = []
            if seg == 1:
                Zb["last_bwt"][par].append(tkmm)
            tk_fe = v.op("tensor_scalar", waits=(tkmm,) + tuple(Zb["last_fev"][q]),
                         out=ffn_ev[:, q, :],
                         in0=PS[q][:, 0:1024],
                         scalar1=bb_cols[:, oc:oc + 1], scalar2=None, op0=ALU.add)
            Zb["last_f_ps"][q].append(tk_fe)
            Zb["last_fev"][q][:] = []
            tk_fp = a.dma(FFN_park.ap()[oc * 128:(oc + 1) * 128,
                                        seg * 1024:seg * 1024 + 1024],
                          ffn_ev[:, q, :], "a_ffn%d" % q, waits=(tk_fe,))
            Zb["last_fev"][q].append(tk_fp)
        return u

    runits = [f for tc in range(NT) for f in (mk_tc(tc), mk_tc_b(tc))] + \
             [mk_bt(e, oc0) for e in range(E) for oc0 in range(0, DC, 8)] + \
             [u_wT, u_wbc, mk_hT(0, DC // 2), mk_hT(DC // 2, DC), u_gT] + \
             [mk_lora(o8) for o8 in range(DC // 8)]
    bunits = [mk_base(oc, seg) for oc in range(DC) for seg in range(NSEG)]
    ri = 0
    for bi, bu in enumerate(bunits):
        bu()
        for _ in range(2 if bi < 8 else 1):
            if ri < len(runits):
                runits[ri]()
                ri += 1
    while ri < len(runits):
        runits[ri]()
        ri += 1

    rec.barrier()

    last_h_ld = [[], []]
    last_ao = [[], []]
    last_pT = [[] for _ in range(8)]
    last_sT = [[], [], [], []]
    last_ot = [[], []]
    last_ivb = []
    last_mg = [[], []]
    last_tl = []
    qh_glob = 0
    pending_tail = []
    park_AO = []
    SLOT = lambda i: PSALL.ap()[:, i * 512:(i + 1) * 512]
    OT = lambda i: PSALL.ap()[:, 2048 + i * 512: 2048 + (i + 1) * 512]
    TL = PSALL.ap()[:, 3072:3584]
    TLA = lambda r: PSALL.ap()[r * 32:r * 32 + 1, 3584:3584 + QH]
    tla_guard = {0: None, 1: None}
    for h in range(H):
        par = h % 2
        sem_h = "s_kv%d" % par
        s.dma(Kh_sb[:, par, :], K_park.ap()[h, :, :], sem_h,
              waits=tuple(last_h_ld[par]))
        s.dma(Qh_sb[:, par, :], Q_park.ap()[h, :, :], sem_h)
        tk_ld = s.dma_tr(Vh_sb[:, par, :, :],
                         VT_park.ap()[h, :, :].rearrange("p (c f) -> p c f", f=128),
                         sem_h)
        last_h_ld[par] = []
        head_toks = []
        tk_pv_last = None
        for qh in range(NQH):
            oi = qh_glob % 2
            ot_cur, ot_nxt = OT(oi), OT(1 - oi)
            qh_glob += 1
            if pending_tail:
                pending_tail.pop(0)()
            npairs = (qh + 1) * QH // 128
            tk_exp = {}
            first = True
            pend = None
            tk_pv_con = None
            tla_st = {"tk": None}

            def emit_pv(p):
                nonlocal first, tk_pv_con
                ptr_, qs_, tk_m_ = p
                cpx = cpv[ptr_]
                for kind in (0, 1):
                    sl_i = ptr_ * 2 + kind
                    koff = kind * TOWN + cpx * 128
                    wts = (tk_m_,)
                    if first:
                        wts = wts + tuple(last_ot[oi]) + (tk_ld,)
                    tk = t.op("matmul", ot_cur[:, qs_:QH],
                              waits=wts,
                              produce=(kind == 1),
                              lhsT=Vh_sb[:, par, koff // 128, :],
                              rhs=pT_sb[:, sl_i, qs_:QH],
                              start=first,
                              stop=(cpx == npairs - 1 and kind == 1))
                    first = False
                tk_pv_con = tk
                for kind in (0, 1):
                    gw = ()
                    if cpx == 0 and kind == 0 and tla_guard[oi] is not None:
                        gw = (tla_guard[oi],)
                    tk_t = t.op("matmul", TLA(oi)[:, qs_:QH], waits=gw,
                                produce=(kind == 1),
                                lhsT=ones_col[:],
                                rhs=pT_sb[:, ptr_ * 2 + kind, qs_:QH],
                                start=(cpx == 0 and kind == 0),
                                stop=(cpx == npairs - 1 and kind == 1))
                tla_st["tk"] = tk_t
                last_pT[ptr_ * 2].append(tk_t)
                last_pT[ptr_ * 2 + 1].append(tk_t)

            cpv = {}
            qsv = {}
            for cp in range(npairs):
                qs = max(0, cp * 128 - qh * QH)
                prng = cp % 2
                ptr = cp % 4
                cpv[ptr] = cp
                qsv[ptr] = qs
                for kind in (0, 1):
                    sl_i = prng * 2 + kind
                    koff = kind * TOWN + cp * 128
                    tk_s = t.op("matmul", SLOT(sl_i)[:, qs:QH],
                                waits=(tk_ld,) + tuple(last_sT[sl_i]),
                                produce=(kind == 1),
                                lhsT=Kh_sb[:, par, koff:koff + 128],
                                rhs=Qh_sb[:, par, qh * QH + qs: (qh + 1) * QH],
                                start=True, stop=True)
                    last_sT[sl_i] = []
                pairview = PSALL.ap()[:, prng * 1024:(prng + 1) * 1024] \
                    .rearrange("p (a q) -> p a q", a=2)[:, :, qs:QH]
                tk_xp = a.op("activation",
                             waits=(tk_s, tk_negC) +
                                   tuple(last_pT[ptr * 2] + last_pT[ptr * 2 + 1]),
                             out=pT_sb[:, ptr * 2:ptr * 2 + 2, qs:QH],
                             in_=pairview,
                             func=AF.Exp, scale=ATT_SCALE, bias=negc_col[:])
                last_pT[ptr * 2] = []
                last_pT[ptr * 2 + 1] = []
                last_sT[prng * 2].append(tk_xp)
                last_sT[prng * 2 + 1].append(tk_xp)
                if cp * 128 >= qh * QH:
                    tk_m = v.op("tensor_tensor", waits=(tk_xp, tk_mp2, tk_mpa),
                                out=pT_sb[:, ptr * 2:ptr * 2 + 2, qs:qs + 128],
                                in0=pT_sb[:, ptr * 2:ptr * 2 + 2, qs:qs + 128],
                                in1=mp2[:], op=ALU.mult)
                else:
                    tk_m = tk_xp
                if cp == 2 and pending_tail:
                    pending_tail.pop(0)()
                if pend is not None:
                    emit_pv(pend)
                pend = (ptr, qs, tk_m)
            emit_pv(pend)
            tk_pv_last = tk_pv_con
            last_ot[oi] = []

            def make_tail(ot_cur=ot_cur, ot_nxt=ot_nxt, oi=oi, qh=qh, par=par,
                          tk_tla=tla_st["tk"], tk_pv_last=tk_pv_last,
                          my_toks=head_toks):
                state = {}

                def part1():
                    tk_ln = a.op("activation", waits=(tk_tla,),
                                 out=invd_s[:, 0:QH],
                                 in_=TLA(oi)[:, 0:QH], func=AF.Ln)
                    tla_guard[oi] = tk_ln
                    state["iv"] = a.op("activation", out=invd_s[:, 0:QH],
                                       in_=invd_s[:, 0:QH], func=AF.Exp, scale=-1.0)

                def part2():
                    nonlocal last_ivb, last_tl
                    tk_bc = t.op("matmul", TL[:, 0:QH],
                                 waits=(state["iv"],) + tuple(last_tl),
                                 produce=True,
                                 lhsT=ones_r128[:], rhs=invd_s[:, 0:QH],
                                 start=True, stop=True)
                    tk_cp = v.op("tensor_copy", waits=(tk_bc,) + tuple(last_ivb),
                                 out=ivb_s[:, 0:QH], in_=TL[:, 0:QH])
                    last_tl = [tk_cp]
                    tk_no = v.op("tensor_tensor",
                                 waits=(tk_pv_last, tk_cp) + tuple(last_ao[par]),
                                 out=ao_sb[:, par, qh * QH:(qh + 1) * QH],
                                 in0=ot_cur[:, 0:QH], in1=ivb_s[:, 0:QH],
                                 op=ALU.mult)
                    last_ivb = [tk_no]
                    last_ao[par] = []
                    last_ot[oi].append(tk_no)
                    my_toks.append(tk_no)
                return [part1, part2]
            pending_tail.extend(make_tail())
        while pending_tail:
            pending_tail.pop(0)()
        tk_pk = a.dma(AO_park.ap()[h, :, :], ao_sb[:, par, :],
                      "a_ao%d" % par, waits=tuple(head_toks))
        last_ao[par] = [tk_pk]
        last_h_ld[par] = [tk_pv_last, tk_pk] + \
                         [tok for l in last_sT for tok in l]
        park_AO.append(tk_pk)
        if h < DC:
            ml = h % 2
            tk_f1 = s.dma(ffn_ld[:, ml, :], FFN_park.ap()[h * 128:(h + 1) * 128, :],
                          "s_mg%d" % ml, waits=tuple(last_mg[ml]))
            tk_f2 = s.dma(lora_ld[:, ml, :], LORA_park.ap()[h * 128:(h + 1) * 128, :],
                          "s_mg%d" % ml)
            tk_ma = v.op("tensor_tensor", waits=(tk_f2,),
                         out=ffn_ld[:, ml, :], in0=ffn_ld[:, ml, :],
                         in1=lora_ld[:, ml, :], op=ALU.add)
            tk_mb = a.dma(FFN_park.ap()[h * 128:(h + 1) * 128, :],
                          ffn_ld[:, ml, :], "a_mg%d" % ml, waits=(tk_ma,))
            last_mg[ml] = [tk_ma, tk_mb]

    rec.barrier()

    last_at = [[], []]
    last_x4 = [[], []]
    last_f4 = [[], []]
    last_st = [[], []]
    last_p4ps = [[], []]
    prev_oh_mm = None
    n_store = 0
    for oh in range(NOH):
        tk_wo = None
        for dc in range(DC):
            tk_wo = g.dma(woh_sb[:, dc, :],
                          wo_in[dc * 128:(dc + 1) * 128, oh * OHALF:(oh + 1) * OHALF],
                          "g_wo",
                          waits=(prev_oh_mm,) if (dc == 0 and prev_oh_mm) else ())
        for tt in range(NT):
            par = tt % 2
            pspair = [PS[0], PS[1]] if par == 0 else [PS[2], PS[3]]
            sem_t = "s_p4%d" % par
            nspl = max(1, H // 8)
            for sp in range(nspl):
                h0, h1 = sp * (H // nspl), (sp + 1) * (H // nspl)
                s.dma(atT_sb[:, par, h0:h1, :],
                      AO_park.ap()[h0:h1, :, tt * 128:(tt + 1) * 128]
                           .rearrange("h p f -> p h f"), sem_t,
                      waits=tuple(last_at[par]) + tuple(last_f4[par]) +
                            tuple(last_x4[par]))
            last_at[par] = []
            s.dma_tr(ffn_p4[:, par, :],
                     FFN_park.ap()[oh * OHALF:(oh + 1) * OHALF,
                                   tt * 128:(tt + 1) * 128], sem_t)
            last_f4[par] = []
            tk_x4 = s.dma(x_p4[:, par, :],
                          x_in[tt * 128:(tt + 1) * 128, oh * OHALF:(oh + 1) * OHALF],
                          sem_t)
            last_x4[par] = []
            tk_at = tk_f4 = tk_x4
            tkmm = None
            for dc in range(DC):
                for off in range(0, OHALF, 512):
                    n = min(512, OHALF - off)
                    tkmm = t.op("matmul",
                                pspair[off // 1024][:, off % 1024: off % 1024 + n],
                                waits=((tk_at, tk_wo) + tuple(last_p4ps[par]))
                                      if (dc == 0 and off == 0) else (),
                                produce=(dc == DC - 1 and off + n >= OHALF),
                                lhsT=atT_sb[:, par, dc, :],
                                rhs=woh_sb[:, dc, off:off + n],
                                start=(dc == 0), stop=(dc == DC - 1))
            last_p4ps[par] = []
            last_at[par].append(tkmm)
            prev_oh_mm = tkmm
            tk_a2 = None
            for i in range((OHALF + 1023) // 1024):
                n = min(1024, OHALF - i * 1024)
                sl = slice(i * 1024, i * 1024 + n)
                tk_a1 = v.op("tensor_tensor",
                             waits=(tkmm, tk_f4) + tuple(last_st[par]),
                             out=out_p4[:, par, sl], in0=pspair[i][:, 0:n],
                             in1=ffn_p4[:, par, sl], op=ALU.add)
                last_p4ps[par].append(tk_a1)
                tk_a2 = v.op("tensor_tensor", waits=(tk_x4,),
                             out=out_p4[:, par, sl], in0=out_p4[:, par, sl],
                             in1=x_p4[:, par, sl], op=ALU.add)
            last_st[par] = []
            last_x4[par].append(tk_a2)
            last_f4[par].append(tk_a2)
            tk_st = a.dma(out_d[tt * 128:(tt + 1) * 128,
                                oh * OHALF:(oh + 1) * OHALF],
                          out_p4[:, par, :], "st%d" % par, waits=(tk_a2,))
            last_st[par] = [tk_st]
            n_store += 1
    s.op("nop", waits=tuple(rec.queue_totals()))

    rec.replay()
    return nc



_PROG_CACHE = {}

def _get_prog():
    if "full" not in _PROG_CACHE:
        _PROG_CACHE["full"] = build_program(CFG_FULL)
    return _PROG_CACHE["full"]


def make_in_maps(cfg, x, wq, wk, wv, wo, base_W, base_b, router_W, router_b,
                 A, B, expert_map, n_cores=8):
    D, S = cfg["D"], cfg["S"]
    NR, E, R = cfg["NR"], cfg["E"], cfg["R"]
    shared = {
        "wq": np.ascontiguousarray(wq, np.float32),
        "wk": np.ascontiguousarray(wk, np.float32),
        "wv": np.ascontiguousarray(wv, np.float32),
        "wo": np.ascontiguousarray(wo, np.float32),
        "base_W": np.ascontiguousarray(base_W, np.float32),
        "base_b": np.ascontiguousarray(base_b, np.float32).reshape(D // 128, 128),
        "router_W": np.ascontiguousarray(router_W, np.float32),
        "router_b": np.ascontiguousarray(router_b, np.float32).reshape(1, NR),
        "A_flat": np.ascontiguousarray(A, np.float32).reshape(E * R, D),
        "B_w": np.ascontiguousarray(B, np.float32).reshape(E * D, R),
        "expert_map": np.ascontiguousarray(expert_map, np.int32).reshape(NR, 1),
    }
    in_maps, orders = [], []
    for c in range(n_cores):
        bc, pc = c // 2, c % 2
        order = [2 * l + pc for l in range(S // 256)] + \
                [2 * l + (1 - pc) for l in range(S // 256)]
        orders.append(order)
        xb = np.ascontiguousarray(
            x[bc].reshape(S // 128, 128, D)[order].reshape(S, D), np.float32)
        pos = (np.asarray(order)[:, None] * 128 +
               np.arange(128)[None, :]).reshape(1, S).astype(np.float32)
        mpart = np.full((128, 128), 0.0 if pc == 0 else 1.0,
                        dtype=ml_dtypes.bfloat16)
        m = dict(shared)
        m["x_perm"] = xb
        m["pos_perm"] = pos
        m["m_part"] = mpart
        in_maps.append(m)
    return in_maps, orders


def assemble_out(cfg, results, orders, Bn):
    D, S = cfg["D"], cfg["S"]
    TOWN = S // 2
    out = np.empty((Bn, S, D), dtype=np.float32)
    for c, r in enumerate(results):
        bc = c // 2
        shard = np.asarray(r["out_own"]).reshape(TOWN // 128, 128, D)
        for l in range(TOWN // 128):
            gt = orders[c][l]
            out[bc, gt * 128:(gt + 1) * 128, :] = shard[l]
    return out


def kernel(L0_attn_hidden_states, L0_attn_wq, L0_attn_wk, L0_attn_wv, L0_attn_wo,
           L0_attn_kq1, L0_attn_kq2, L0_attn_kk1, L0_attn_kk2, L0_attn_scalingD,
           L0_attn_interaction_k, L0_attn_scalingD2, L0_ffn_base_W, L0_ffn_base_b,
           L0_ffn_router_W, L0_ffn_router_b, L0_ffn_A, L0_ffn_B, L0_ffn_expert_map):
    cfg = CFG_FULL
    x = np.ascontiguousarray(L0_attn_hidden_states, dtype=np.float32)
    nc = _get_prog()
    in_maps, orders = make_in_maps(
        cfg, x, L0_attn_wq, L0_attn_wk, L0_attn_wv, L0_attn_wo,
        L0_ffn_base_W, L0_ffn_base_b, L0_ffn_router_W, L0_ffn_router_b,
        L0_ffn_A, L0_ffn_B, L0_ffn_expert_map)
    res = run_bass_kernel_spmd(nc, in_maps, core_ids=list(range(8)))
    return assemble_out(cfg, res.results, orders, x.shape[0])


# version 37
# speedup vs baseline: 1.0844x; 1.0844x over previous
import numpy as np
import ml_dtypes
import concourse.bass as bass
import concourse.mybir as mybir
from concourse.bass_utils import run_bass_kernel_spmd

F32 = mybir.dt.float32
BF16 = mybir.dt.bfloat16
I32 = mybir.dt.int32
AF = mybir.ActivationFunctionType
ALU = mybir.AluOpType
AX = mybir.AxisListType

MAGIC = float(np.float32(12582912.0))
TWO_PI = float(np.float32(2 * np.pi))
INV_2PI = float(np.float32(1.0 / (2 * np.pi)))
PI_CLAMP = 3.14159203
HALF_PI = float(np.float32(np.pi / 2))
EXP_SHIFT = 16.0
LN_EPS = 1e-5

CFG_FULL = dict(D=4096, S=4096, H=32, NR=15, E=8, R=16)



class _Eng:
    def __init__(self, rec, name, chain_sem, chained):
        self._rec, self._name = rec, name
        self._chain, self._chained = chain_sem, chained
        self._n = 0
        self._waited = {}
        self.items = []

    def wait(self, token):
        if token is None:
            return
        sem, val = token
        key = id(sem)
        if self._waited.get(key, -1) >= val:
            return
        self._waited[key] = val
        self.items.append(("wait", sem, val))

    def op(self, meth, *args, waits=(), produce=False, inc=None, **kw):
        for tk in waits:
            self.wait(tk)
        if self._chained and self._n:
            self.wait((self._chain, self._n))
        tok, incs = None, None
        if inc is not None:
            sem, by = inc
            c = self._rec.sem_counts
            c[id(sem)] = c.get(id(sem), 0) + by
            tok, incs = (sem, c[id(sem)]), (sem, by)
        elif self._chained or produce:
            self._n += 1
            tok, incs = (self._chain, self._n), (self._chain, 1)
        self.items.append(("call", meth, args, kw, incs))
        return tok

    def dma(self, out, in_, sem, waits=(), **kw):
        h = self._rec.dsem(sem)
        return self.op("dma_start", waits=waits, inc=(h, 16), out=out, in_=in_, **kw)

    def dma_tr(self, out, in_, sem, waits=()):
        h = self._rec.dsem(sem)
        return self.op("dma_start_transpose", out, in_, waits=waits, inc=(h, 16))


class Rec:
    def __init__(self, nc):
        self.nc = nc
        self.sem_counts = {}
        self.dsems = {}
        self.t = _Eng(self, "tensor", nc.alloc_semaphore("ch_pe"), chained=False)
        self.v = _Eng(self, "vector", nc.alloc_semaphore("ch_dve"), chained=True)
        self.a = _Eng(self, "scalar", nc.alloc_semaphore("ch_act"), chained=True)
        self.g = _Eng(self, "gpsimd", nc.alloc_semaphore("ch_pool"), chained=True)
        self.s = _Eng(self, "sync", nc.alloc_semaphore("ch_sp"), chained=False)
        self.engs = {"tensor": self.t, "vector": self.v, "scalar": self.a,
                     "gpsimd": self.g, "sync": self.s}

    def dsem(self, name):
        if name not in self.dsems:
            self.dsems[name] = self.nc.alloc_semaphore("dq_" + name)
        return self.dsems[name]

    def queue_totals(self):
        return [(h, self.sem_counts.get(id(h), 0)) for h in self.dsems.values()
                if self.sem_counts.get(id(h), 0) > 0]

    def barrier(self, tokens=()):
        alltk = list(tokens) + self.queue_totals() + \
                [(e._chain, e._n) for e in self.engs.values() if e._n > 0]
        for eng in self.engs.values():
            for tk in alltk:
                eng.wait(tk)

    def replay(self):
        with self.nc.Block() as block:
            for name, eng in self.engs.items():
                def make(items):
                    def run(e):
                        for it in items:
                            if it[0] == "wait":
                                e.wait_ge(it[1], it[2])
                            else:
                                _, meth, args, kw, incs = it
                                r = getattr(e, meth)(*args, **kw)
                                if incs is not None:
                                    r.then_inc(incs[0], incs[1])
                    return run
                getattr(block, name)(make(eng.items))



def build_program(cfg):
    D, S, H, NR, E, R = (cfg[k] for k in ("D", "S", "H", "NR", "E", "R"))
    HD = 128
    assert D == H * HD
    DC = D // 128
    TOWN = S // 2
    NT = TOWN // 128
    NTB = S // 128
    QTR = TOWN // 2
    NQT = QTR // 128
    assert QTR % 512 == 0, "psum zero regions are bank-granular"

    ER = E * R
    assert ER == 128
    VW = min(512, D)
    NVQ = D // VW
    QH = min(512, TOWN)
    NQH = TOWN // QH
    NSEG = (TOWN + 1023) // 1024
    OHALF = min(cfg.get('OHALF', 2048), D)
    NOH = D // OHALF
    ATT_SCALE = float(1.0 / np.sqrt(HD))

    nc = bass.Bass("TRN2", target_bir_lowering=False)

    x_in = nc.dram_tensor("x_perm", [S, D], F32, kind="ExternalInput").ap()
    pos_in = nc.dram_tensor("pos_perm", [1, S], F32, kind="ExternalInput").ap()
    mpart_in = nc.dram_tensor("m_part", [128, 128], BF16, kind="ExternalInput").ap()
    wq_in = nc.dram_tensor("wq", [D, D], F32, kind="ExternalInput").ap()
    wk_in = nc.dram_tensor("wk", [D, D], F32, kind="ExternalInput").ap()
    wv_in = nc.dram_tensor("wv", [D, D], F32, kind="ExternalInput").ap()
    wo_in = nc.dram_tensor("wo", [D, D], F32, kind="ExternalInput").ap()
    bW_in = nc.dram_tensor("base_W", [D, D], F32, kind="ExternalInput").ap()
    bb_in = nc.dram_tensor("base_b", [DC, 128], F32, kind="ExternalInput").ap()
    rW_in = nc.dram_tensor("router_W", [NR, D], F32, kind="ExternalInput").ap()
    rb_in = nc.dram_tensor("router_b", [1, NR], F32, kind="ExternalInput").ap()
    A_in = nc.dram_tensor("A_flat", [ER, D], F32, kind="ExternalInput").ap()
    B_in = nc.dram_tensor("B_w", [E * D, R], F32, kind="ExternalInput").ap()
    emap_in = nc.dram_tensor("expert_map", [NR, 1], I32, kind="ExternalInput").ap()
    out_d = nc.dram_tensor("out_own", [TOWN, D], F32, kind="ExternalOutput").ap()

    ident_bf_d = nc.inline_tensor(np.eye(128, dtype=ml_dtypes.bfloat16), "c_identb").ap()
    ident_f_d = nc.inline_tensor(np.eye(128, dtype=np.float32), "c_identf").ap()
    invf_np = (10000.0 ** (-(np.arange(128) % 64) * 2.0 / HD)).astype(np.float32)
    invf_d = nc.inline_tensor(invf_np.reshape(1, 128), "c_invf").ap()
    sgn_d = nc.inline_tensor(
        np.where(np.arange(128) < 64, 1.0, -1.0).astype(np.float32).reshape(128, 1),
        "c_sgn").ap()
    diag_d = nc.inline_tensor(
        (np.arange(128)[None, :] >= np.arange(128)[:, None]).astype(ml_dtypes.bfloat16),
        "c_diag").ap()
    exp16_d = nc.inline_tensor(
        (np.arange(ER)[None, :] // R == np.arange(E)[:, None]).astype(ml_dtypes.bfloat16),
        "c_exp16").ap()
    eiota_d = nc.inline_tensor(
        np.tile(np.arange(E, dtype=np.float32), (NR, 1)), "c_eiota").ap()

    wq_t = nc.dram_tensor("wq_t", [H, 128, DC, 128], BF16, kind="Internal")
    wk_t = nc.dram_tensor("wk_t", [H, 128, DC, 128], BF16, kind="Internal")
    wv_t = nc.dram_tensor("wv_t", [H, 128, DC, 128], BF16, kind="Internal")
    bW_b = nc.dram_tensor("bW_b", [D, D], BF16, kind="Internal")
    B_b = nc.dram_tensor("B_b", [E * D, R], BF16, kind="Internal")
    BT_park = nc.dram_tensor("BT_park", [ER, D], BF16, kind="Internal")
    K_park = nc.dram_tensor("K_park", [H, 128, S], BF16, kind="Internal")
    Q_park = nc.dram_tensor("Q_park", [H, 128, TOWN], BF16, kind="Internal")
    VT_park = nc.dram_tensor("VT_park", [H, 128, S], BF16, kind="Internal")
    FFN_park = nc.dram_tensor("FFN_park", [D, TOWN], BF16, kind="Internal")
    LORA_park = nc.dram_tensor("LORA_park", [D, TOWN], BF16, kind="Internal")
    A_db = nc.dram_tensor("A_db", [ER, D], BF16, kind="Internal")
    AO_park = nc.dram_tensor("AO_park", [H, 128, TOWN], BF16, kind="Internal")

    rec = Rec(nc)
    t, v, a, g, s = rec.t, rec.v, rec.a, rec.g, rec.s

    PSALL = nc.alloc_psum_tensor("psall", [128, 4096], F32)
    PS = [PSALL.ap()[:, i * 1024:(i + 1) * 1024] for i in range(4)]
    psA, psB, psC, psD = PS

    def sbt(name, shape, dt=F32):
        return nc.alloc_sbuf_tensor(name, shape, dt)
    ident_bs = sbt("ident_bs", [128, 128], BF16)
    ident_fs = sbt("ident_fs", [128, 128], F32)
    invf_s = sbt("invf_s", [1, 128], F32)
    sgn_s = sbt("sgn_s", [128, 1], F32)
    diag_s = sbt("diag_s", [128, 128], BF16)
    mpart_s = sbt("mpart_s", [128, 128], BF16)
    mp2 = sbt("mp2", [128, 2, 128], BF16)
    exp16_s = sbt("exp16_s", [E, 128], BF16)
    eio_s = sbt("eio_s", [NR, E], F32)
    emap_i = sbt("emap_i", [NR, 1], I32)
    emap_f = sbt("emap_f", [NR, 1], F32)
    sel_f = sbt("sel_f", [NR, E], F32)
    sel_b = sbt("sel_b", [NR, E], BF16)
    ones_col = sbt("ones_col", [128, 1], BF16)
    negc_col = sbt("negc_col", [128, 1], F32)
    ones_r128 = sbt("ones_r128", [1, 128], F32)
    rb_row = sbt("rb_row", [1, NR], F32)
    bb_dc = sbt("bb_dc", [DC, 128], F32)
    bb_cols = sbt("bb_cols", [128, DC], F32)
    stats = sbt("stats", [128, 8], F32)
    posq = sbt("posq", [1, max(QTR, 128)], F32)
    lgs = sbt("lgs", [128, NR], F32)
    cur_s = sbt("cur_s", [128, NR], F32)
    eq_s = sbt("eq_s", [128, NR], F32)
    pi_b = sbt("pi_b", [128, NR], BF16)
    lcol = sbt("lcol", [128, 6], F32)
    Bt_in = sbt("Bt_in", [128, 8, R], BF16)
    Bt_ev = sbt("Bt_ev", [R, 8 * 128], BF16)
    rw_pad = sbt("rw_pad", [16, D], BF16)
    invd_s = sbt("invd_s", [1, 1024], F32)

    AR_BYTES = 189696
    arena = nc.alloc_sbuf_tensor("arena", [128, AR_BYTES // 2], BF16)

    class Alloc:
        def __init__(self, off=0):
            self.off = off
        def take(self, shape, dt=BF16):
            esz = mybir.dt.size(dt)
            per_p = int(np.prod(shape[1:])) * esz
            o = (self.off + 31) // 32 * 32
            self.off = o + per_p
            assert self.off <= AR_BYTES, f"arena overflow {self.off}"
            flat = arena.ap().bitcast(dt)[0:shape[0], o // esz: (o + per_p) // esz]
            if len(shape) == 2:
                return flat
            dims = {f"d{i}": shape[i] for i in range(1, len(shape))}
            pat = ("p (" + " ".join(dims) + ") -> p " + " ".join(dims))
            return flat.rearrange(pat, **dims)

    al = Alloc()
    xnT = [al.take([128, DC, QTR]), al.take([128, DC, QTR])]
    P2_BASE = al.off
    wg_sb = al.take([128, 2, DC, 128])
    xbuf = al.take([128, D], F32)
    xn_tok = al.take([128, D])
    cosq = al.take([128, QTR])
    sinq = al.take([128, QTR])
    qk_e = al.take([128, 2, QTR])
    th_t = al.take([128, QTR], F32)
    rot_t1 = th_t.bitcast(BF16)[:, 0:QTR]

    al2 = Alloc(P2_BASE)
    bWT_t = al2.take([128, 2, DC, 128])
    AT_s = al2.take([128, DC, 128])
    rwT_s = al2.take([128, DC, 16])
    BT_s = AT_s.rearrange("p a b -> p (a b)")
    piT = al2.take([NR, TOWN])
    wT_s = al2.take([E, TOWN])
    wbc_s = al2.take([128, TOWN])
    gT_s = al2.take([128, TOWN])
    ffn_ev = al2.take([128, 2, 1024])
    lora_ev = al2.take([128, 2, 1024])
    bw_row = al2.take([128, D])

    al3 = Alloc()
    Kh_sb = al3.take([128, 2, S])
    Vh_sb = al3.take([128, 2, NTB, 128])
    Qh_sb = al3.take([128, 2, TOWN])
    pT_sb = al3.take([128, 8, QH])
    ao_sb = al3.take([128, 2, TOWN])
    dacc_g = al3.take([128, QH], F32)
    dacc_v = al3.take([128, QH], F32)
    ivb_s = al3.take([128, QH], F32)
    ffn_ld = al3.take([128, 2, TOWN])
    lora_ld = al3.take([128, 2, TOWN])

    al4 = Alloc()
    woh_sb = al4.take([128, DC, OHALF])
    atT_sb = al4.take([128, 2, DC, 128])
    ffn_p4 = al4.take([128, 2, OHALF])
    x_p4 = al4.take([128, 2, OHALF], F32)
    out_p4 = al4.take([128, 2, OHALF], F32)

    for dst, srcp in [(ident_bs[:], ident_bf_d), (ident_fs[:], ident_f_d),
                      (invf_s[:], invf_d), (sgn_s[:], sgn_d), (diag_s[:], diag_d),
                      (mpart_s[:], mpart_in), (exp16_s[:], exp16_d),
                      (eio_s[:], eiota_d), (emap_i[:], emap_in),
                      (rb_row[:], rb_in), (bb_dc[:], bb_in)]:
        tk_consts = s.dma(dst, srcp, "const")
    tk_idb = tk_idf = tk_invf = tk_sgn = tk_diag = tk_mp = tk_consts
    tk_x16 = tk_eio = tk_emap = tk_rb = tk_bb2 = tk_consts
    tk_mpa = v.op("tensor_copy", waits=(tk_diag,), out=mp2[:, 0, :], in_=diag_s[:])
    tk_mp2 = v.op("tensor_copy", waits=(tk_mp,), out=mp2[:, 1, :], in_=mpart_s[:])
    v.op("memset", ones_col[:], 1.0)
    tk_negC = v.op("memset", negc_col[:], -EXP_SHIFT)
    tk_ones = v.op("memset", ones_r128[:], 1.0)

    def cast_tiled(dst, src, sem, first_solo=False):
        if first_solo:
            groups = [[0], [1, 2, 3]] + \
                     [list(range(hg, hg + 4)) for hg in range(4, H, 4)]
        else:
            groups = [list(range(hg, hg + 4)) for hg in range(0, H, 4)]
        gtoks = {}
        step = max(1, 8)
        for gi, hs in enumerate(groups):
            tk = None
            for h in hs:
                for dq in range(0, DC, step):
                    dn = min(step, DC - dq)
                    tk = g.dma(dst.ap()[h, :, dq:dq + dn, :],
                               src.rearrange("(dc p) (h j) -> h p dc j",
                                             p=128, j=128)
                                  [h, :, dq:dq + dn, :],
                               "%s_%d" % (sem, gi))
            for h in hs:
                gtoks[h] = tk
        return [gtoks[h] for h in range(H)]
    tks_wk = cast_tiled(wk_t, wk_in, "g_wk", first_solo=True)
    tks_wv = cast_tiled(wv_t, wv_in, "g_wv")
    tks_wq = cast_tiled(wq_t, wq_in, "g_wq")
    tk_bW_b = None
    for dc in range(DC):
        tk_bW_b = g.dma(bW_b.ap()[dc * 128:(dc + 1) * 128, :],
                        bW_in[dc * 128:(dc + 1) * 128, :], "g_bW")
    tk_B_b = g.dma(B_b.ap()[:, :], B_in, "g_B")

    tk_bbT = t.op("transpose", psA[0:128, 0:DC], produce=True,
                  waits=(tk_bb2, tk_idf),
                  in_=bb_dc[:], identity=ident_fs[0:DC, 0:DC])
    tk_bbc = v.op("tensor_copy", waits=(tk_bbT,), out=bb_cols[:],
                  in_=psA[0:128, 0:DC])

    v.op("tensor_copy", waits=(tk_emap,), out=emap_f[:], in_=emap_i[:])
    v.op("tensor_scalar", waits=(tk_eio,), out=sel_f[:], in0=eio_s[:],
         scalar1=emap_f[:], scalar2=None, op0=ALU.is_equal)
    tk_selb = v.op("tensor_copy", out=sel_b[:], in_=sel_f[:])

    last_xbuf = []
    last_xn = []
    last_stat = []
    last_xnT = [[], []]
    last_wg = [[], []]
    last_qk = [[], []]
    last_kq_ps = [[tk_bbc], []]
    last_tab = []
    last_thbuf = []
    last_rot = []
    last_posrd = []
    park_K, park_Q, park_V = [], [], []

    def ln_tile(qi, slot, tc):
        nonlocal last_xbuf, last_xn, last_stat
        buf = xnT[slot]
        row0 = qi * QTR + tc * 128
        tk_x = s.dma(xbuf[:], x_in[row0:row0 + 128, :], "s_x",
                     waits=tuple(last_xbuf))
        last_xbuf = []
        tk_sum = v.op("reduce_sum", waits=(tk_x,), out=stats[:, 0:1],
                      in_=xbuf[:], axis=AX.X)
        tk_sq = a.op("activation", waits=(tk_x,) + tuple(last_xn) + tuple(last_stat),
                     out=xn_tok[:],
                     in_=xbuf[:], func=AF.Square, accum_out=stats[:, 1:2])
        v.op("tensor_scalar", waits=(tk_sum,), out=stats[:, 2:3],
             in0=stats[:, 0:1], scalar1=float(1.0 / D), scalar2=None, op0=ALU.mult)
        v.op("tensor_tensor", out=stats[:, 3:4], in0=stats[:, 2:3],
             in1=stats[:, 2:3], op=ALU.mult)
        v.op("tensor_scalar", waits=(tk_sq,), out=stats[:, 4:5],
             in0=stats[:, 1:2], scalar1=float(1.0 / D), scalar2=None, op0=ALU.mult)
        v.op("tensor_tensor", out=stats[:, 4:5], in0=stats[:, 4:5],
             in1=stats[:, 3:4], op=ALU.subtract)
        tk_ve = v.op("tensor_scalar", out=stats[:, 5:6], in0=stats[:, 4:5],
                     scalar1=float(LN_EPS), scalar2=None, op0=ALU.add)
        last_stat = [tk_ve]
        a.op("activation", waits=(tk_ve,), out=stats[:, 6:7], in_=stats[:, 5:6],
             func=AF.Ln)
        tk_rs = a.op("activation", out=stats[:, 7:8], in_=stats[:, 6:7],
                     func=AF.Exp, scale=-0.5)
        tk_xn = v.op("tensor_scalar", waits=(tk_rs,), out=xn_tok[:], in0=xbuf[:],
                     scalar1=stats[:, 2:3], scalar2=stats[:, 7:8],
                     op0=ALU.subtract, op1=ALU.mult)
        last_xbuf = [tk_xn]
        tk_tr = a.dma_tr(buf[:, :, tc * 128:(tc + 1) * 128],
                         xn_tok.rearrange("p (c f) -> p c f", f=128),
                         "s_tr%d" % slot,
                         waits=tuple([tk_xn] + last_xnT[slot]))
        last_xnT[slot] = []
        last_xn = [tk_tr]
        return tk_tr

    def rotary_tables(qi):
        nonlocal last_tab, last_thbuf
        tk_p = s.dma(posq[:, 0:QTR], pos_in[:, qi * QTR:(qi + 1) * QTR], "s_pos",
                     waits=tuple(last_posrd))
        tk_th = None
        for off in range(0, QTR, 512):
            n = min(512, QTR - off)
            tk_th = t.op("matmul", psC[:, off:off + n],
                         waits=((tk_p, tk_invf) + tuple(last_tab)) if off == 0 else (),
                         produce=(off + n >= QTR),
                         lhsT=invf_s[:], rhs=posq[:, off:off + n],
                         start=True, stop=True)
        last_posrd[:] = [tk_th]
        last_tab = []
        sins = []
        for phase, dst in (("sin", sinq), ("cos", cosq)):
            shift = 0.0 if phase == "sin" else 0.25
            psh = 0.0 if phase == "sin" else HALF_PI
            v.op("tensor_scalar", waits=(tk_th,) + tuple(last_thbuf), out=th_t[:],
                 in0=psC[:, 0:QTR], scalar1=INV_2PI, scalar2=shift,
                 op0=ALU.mult, op1=ALU.add)
            last_thbuf = []
            v.op("tensor_scalar", out=th_t[:], in0=th_t[:],
                 scalar1=MAGIC, scalar2=MAGIC, op0=ALU.add, op1=ALU.subtract)
            v.op("tensor_scalar", out=th_t[:], in0=th_t[:],
                 scalar1=-TWO_PI, scalar2=None, op0=ALU.mult)
            tk2 = v.op("scalar_tensor_tensor", out=th_t[:], in0=psC[:, 0:QTR],
                       scalar=psh, in1=th_t[:], op0=ALU.add, op1=ALU.add)
            last_tab.append(tk2)
            tk_cl = v.op("tensor_scalar", out=th_t[:], in0=th_t[:],
                         scalar1=PI_CLAMP, scalar2=-PI_CLAMP, op0=ALU.min, op1=ALU.max)
            tk_sin = a.op("activation", out=dst[:, 0:QTR], in_=th_t[:], func=AF.Sin,
                          waits=(tk_cl,) + tuple(last_rot))
            sins.append(tk_sin)
            last_thbuf.append(tk_sin)
        tk_fold = v.op("tensor_scalar", waits=(sins[0], tk_sgn), out=sinq[:, 0:QTR],
                       in0=sinq[:, 0:QTR], scalar1=sgn_s[:], scalar2=None,
                       op0=ALU.mult)
        return (sins[1], tk_fold)

    def rotary_apply(sl, tk_src, tk_tab):
        w = (tk_src,) + tk_tab
        v.op("tensor_tensor", waits=w, out=rot_t1[0:64, 0:QTR],
             in0=sl[64:128, :], in1=sinq[64:128, 0:QTR], op=ALU.mult)
        v.op("tensor_tensor", out=rot_t1[64:128, 0:QTR],
             in0=sl[0:64, :], in1=sinq[0:64, 0:QTR], op=ALU.mult)
        v.op("tensor_tensor", out=sl, in0=sl, in1=cosq[:, 0:QTR], op=ALU.mult)
        tk = v.op("tensor_tensor", out=sl, in0=sl, in1=rot_t1[:, 0:QTR], op=ALU.add)
        last_rot[:] = [tk]
        return tk

    QORDER = [2, 3, 0, 1]
    ln_toks = {QORDER[0]: [ln_tile(QORDER[0], 0, tc) for tc in range(NQT)]}
    for oi, qi in enumerate(QORDER):
        slot = oi % 2
        tr_toks = ln_toks[qi]
        buf = xnT[slot]
        tk_tab = rotary_tables(qi)
        own = qi < 2
        nxt = QORDER[oi + 1] if oi + 1 < len(QORDER) else None
        ln_next = []
        nheads_tot = H * (2 if own else 1)
        ln_every = max(1, nheads_tot // NQT)
        hcount = 0

        def maybe_ln():
            nonlocal hcount
            hcount += 1
            if nxt is not None and hcount % ln_every == 0 and len(ln_next) < NQT:
                ln_next.append(ln_tile(nxt, (oi + 1) % 2, len(ln_next)))

        projs = [("K", wk_t, tks_wk, K_park, qi * QTR),
                 ("V", wv_t, tks_wv, VT_park, qi * QTR)]
        if own:
            projs.append(("Q", wq_t, tks_wq, Q_park, qi * QTR))
        for pname, w_t, wtoks, park, col0 in projs:
            pk_list = {"K": park_K, "Q": park_Q, "V": park_V}[pname]
            for h in range(H):
                pslot = h % 2
                pbuf = psA if pslot == 0 else psB
                tk_w = s.dma(wg_sb[:, pslot, :, :],
                             w_t.ap()[h, :, :, :], "s_wg%d" % pslot,
                             waits=tuple(last_wg[pslot]) + (wtoks[h],))
                last_wg[pslot] = []
                tkmm = None
                for dc in range(DC):
                    for off in range(0, QTR, 512):
                        n = min(512, QTR - off)
                        tkmm = t.op(
                            "matmul", pbuf[:, off:off + n],
                            waits=((tk_w, tr_toks[-1]) + tuple(last_kq_ps[pslot]))
                                  if (dc == 0 and off == 0) else (),
                            produce=(dc == DC - 1 and off + n >= QTR),
                            lhsT=wg_sb[:, pslot, dc, :],
                            rhs=buf[:, dc, off:off + n],
                            start=(dc == 0), stop=(dc == DC - 1))
                last_kq_ps[pslot] = []
                last_wg[pslot].append(tkmm)
                sl = qk_e[:, pslot, :]
                tk_ev = a.op("activation", waits=(tkmm,) + tuple(last_qk[pslot]),
                             out=sl, in_=pbuf[:, 0:QTR], func=AF.Copy)
                last_kq_ps[pslot].append(tk_ev)
                tk_ro = tk_ev if pname == "V" else rotary_apply(sl, tk_ev, tk_tab)
                tk_pk = a.dma(park.ap()[h, :, col0:col0 + QTR], sl,
                              "a_pk%d" % pslot, waits=(tk_ro,))
                last_qk[pslot] = [tk_pk]
                pk_list.append(tk_pk)
                maybe_ln()
        while nxt is not None and len(ln_next) < NQT:
            ln_next.append(ln_tile(nxt, (oi + 1) % 2, len(ln_next)))
        if nxt is not None:
            ln_toks[nxt] = ln_next

        last_xnT[slot] = [tkmm]

    rec.barrier()

    tk_A_db = g.dma(A_db.ap(), A_in, "g_A")
    tk_rwz = v.op("memset", rw_pad[:], 0.0)
    tk_rwl = g.dma(rw_pad[0:NR, :], rW_in, "g_rw", waits=(tk_rwz,))
    tk_rwT = s.dma_tr(rwT_s[:], rw_pad.ap().rearrange("p (c f) -> p c f", f=128),
                      "s_p2tr", waits=(tk_rwl,))
    tk_AT = s.dma_tr(AT_s[:], A_db.ap().rearrange("p (c f) -> p c f", f=128),
                     "s_p2at", waits=(tk_A_db,))

    last_r_ps = [[], []]
    Z = {"tk_pe": None, "tk_pi": None, "tk_pt_prev": None,
         "tk_BT": None, "tk_btr_prev": None, "tk_bev_prev": None,
         "tk_wT": None, "tk_wTe": None, "tk_wbc": None, "tk_wbce": None,
         "tk_hT": None, "tk_gT": None, "tk_BTs": None,
         "last_lo": [[], []], "last_lev": [[], []]}

    def mk_tc(tc):
        def u():
            pslot = tc % 2
            pbuf = PS[2] if pslot == 0 else PS[3]
            for dc in range(DC):
                bufq = xnT[(tc * 128) // QTR]
                t.op("matmul", pbuf[:, 0:NR],
                     waits=((tk_rwT,) + tuple(last_r_ps[pslot])) if dc == 0 else (),
                     lhsT=bufq[:, dc, (tc * 128) % QTR:(tc * 128) % QTR + 128],
                     rhs=rwT_s[:, dc, 0:NR],
                     start=(dc == 0), stop=False)
            last_r_ps[pslot][:] = []
            tkmm = t.op("matmul", pbuf[:, 0:NR], produce=True,
                        waits=(tk_rb, tk_ones),
                        lhsT=ones_r128[:], rhs=rb_row[:], start=False, stop=True)
            tk_c = v.op("tensor_copy", waits=(tkmm,), out=lgs[:], in_=pbuf[:, 0:NR])
            last_r_ps[pslot].append(tk_c)
            v.op("reduce_max", out=lcol[:, 0:1], in_=lgs[:], axis=AX.X)
            tk_nm = v.op("tensor_scalar", out=lcol[:, 1:2], in0=lcol[:, 0:1],
                         scalar1=-1.0, scalar2=None, op0=ALU.mult)
            tk_e = a.op("activation", waits=(tk_nm,), out=lgs[:], in_=lgs[:],
                        func=AF.Exp, bias=lcol[:, 1:2], accum_out=lcol[:, 2:3])
            v.op("reciprocal", waits=(tk_e,), out=lcol[:, 3:4], in_=lcol[:, 2:3])
            v.op("tensor_scalar", out=lgs[:], in0=lgs[:],
                 scalar1=lcol[:, 3:4], scalar2=None, op0=ALU.mult)
            v.op("tensor_copy", out=cur_s[:], in_=lgs[:])
            for _ in range(E - 1):
                v.op("reduce_max", out=lcol[:, 4:5], in_=cur_s[:], axis=AX.X)
                v.op("tensor_scalar", out=eq_s[:], in0=cur_s[:],
                     scalar1=lcol[:, 4:5], scalar2=None, op0=ALU.is_equal)
                v.op("scalar_tensor_tensor", out=cur_s[:], in0=eq_s[:],
                     scalar=-2.0, in1=cur_s[:], op0=ALU.mult, op1=ALU.add)
            v.op("reduce_max", out=lcol[:, 5:6], in_=cur_s[:], axis=AX.X)
            v.op("tensor_scalar", out=eq_s[:], in0=lgs[:],
                 scalar1=lcol[:, 5:6], scalar2=None, op0=ALU.is_ge)
            tk_pi = v.op("tensor_tensor",
                         waits=(Z["tk_pt_prev"],) if Z["tk_pt_prev"] else (),
                         out=pi_b[:], in0=lgs[:], in1=eq_s[:],
                         op=ALU.mult)
            Z["tk_pi"] = tk_pi
        return u

    def mk_tc_b(tc):
        def u():
            pslot = tc % 2
            pbuf = PS[2] if pslot == 0 else PS[3]
            tk_pt = t.op("transpose", pbuf.bitcast(BF16)[0:NR, 1024:1024 + 128],
                         produce=True, waits=(Z["tk_pi"], tk_idb),
                         in_=pi_b[:], identity=ident_bs[:])
            Z["tk_pt_prev"] = tk_pt
            tk_pe = a.op("activation", waits=(tk_pt,),
                         out=piT[:, tc * 128:(tc + 1) * 128],
                         in_=pbuf.bitcast(BF16)[0:NR, 1024:1024 + 128],
                         func=AF.Copy)
            Z["tk_pe"] = tk_pe
            last_r_ps[pslot].append(tk_pe)
        return u

    def mk_bt(e, oc0):
        def u():
            nb = min(8, DC - oc0)
            tk_bl = g.dma(Bt_in[:, 0:nb, :],
                          B_b.ap()[e * D + oc0 * 128: e * D + (oc0 + nb) * 128, :]
                             .rearrange("(c p) r -> p c r", p=128),
                          "g_bt",
                          waits=(tk_B_b,) +
                                ((Z["tk_btr_prev"],) if Z["tk_btr_prev"] else ()))
            tk_btr = None
            for i in range(nb):
                tk_btr = t.op("transpose",
                              PS[2].bitcast(BF16)[0:R, 1024 + i * 128:1152 + i * 128],
                              produce=(i == nb - 1),
                              waits=((tk_bl, tk_idb) + tuple(last_r_ps[0]) +
                                     ((Z["tk_bev_prev"],) if Z["tk_bev_prev"] else ()))
                                    if i == 0 else (),
                              in_=Bt_in[:, i, :], identity=ident_bs[:])
            Z["tk_btr_prev"] = tk_btr
            last_r_ps[0][:] = []
            tk_bev = a.op("activation",
                          waits=(tk_btr,) + ((Z["tk_BT"],) if Z["tk_BT"] else ()),
                          out=Bt_ev[:, 0:nb * 128],
                          in_=PS[2].bitcast(BF16)[0:R, 1024:1024 + nb * 128],
                          func=AF.Copy, scale=2.0)
            Z["tk_bev_prev"] = tk_bev
            last_r_ps[0].append(tk_bev)
            Z["tk_BT"] = g.dma(BT_park.ap()[e * R:(e + 1) * R,
                                            oc0 * 128:(oc0 + nb) * 128],
                               Bt_ev[:, 0:nb * 128], "g_btp", waits=(tk_bev,))
        return u

    def u_wT():
        tk = None
        w0 = (tk_selb,) + tuple(last_r_ps[0]) + tuple(last_r_ps[1]) + \
             ((Z["tk_bev_prev"],) if Z["tk_bev_prev"] else ()) + \
             ((Z["tk_pe"],) if Z["tk_pe"] else ())
        for off in range(0, TOWN, 512):
            n = min(512, TOWN - off)
            tk = t.op("matmul", PS[2 + off // 1024][0:E, off % 1024: off % 1024 + n],
                      waits=w0 if off == 0 else (),
                      produce=(off + n >= TOWN),
                      lhsT=sel_b[:], rhs=piT[:, off:off + n], start=True, stop=True)
        Z["tk_wT"] = tk
        tke = None
        for i in range(NSEG):
            n = min(1024, TOWN - i * 1024)
            tke = a.op("activation", waits=(Z["tk_wT"],),
                       out=wT_s[:, i * 1024:i * 1024 + n],
                       in_=PS[2 + i][0:E, 0:n], func=AF.Copy)
        Z["tk_wTe"] = tke

    def u_wbc():
        tk = None
        for off in range(0, TOWN, 512):
            n = min(512, TOWN - off)
            tk = t.op("matmul", PS[2 + off // 1024][:, off % 1024: off % 1024 + n],
                      waits=(tk_x16, Z["tk_wTe"]) if off == 0 else (),
                      produce=(off + n >= TOWN),
                      lhsT=exp16_s[:], rhs=wT_s[:, off:off + n], start=True, stop=True)
        Z["tk_wbc"] = tk
        tke = None
        for i in range(NSEG):
            n = min(1024, TOWN - i * 1024)
            tke = a.op("activation", waits=(Z["tk_wbc"],),
                       out=wbc_s[:, i * 1024:i * 1024 + n],
                       in_=PS[2 + i][:, 0:n], func=AF.Copy)
        Z["tk_wbce"] = tke

    def mk_hT(dc0, dc1):
        def u():
            tk = Z["tk_hT"]
            for dc in range(dc0, dc1):
                c = 0
                while c < TOWN:
                    n = min(TOWN - c, 512 - (c % 512), QTR - (c % QTR))
                    tk = t.op("matmul", PS[2 + c // 1024][:, c % 1024: c % 1024 + n],
                              waits=(tk_AT, Z["tk_wbce"])
                                    if (dc == 0 and c == 0) else (),
                              produce=(dc == DC - 1 and c + n >= TOWN),
                              lhsT=AT_s[:, dc, :],
                              rhs=xnT[c // QTR][:, dc, c % QTR: c % QTR + n],
                              start=(dc == 0), stop=(dc == DC - 1))
                    c += n
            Z["tk_hT"] = tk
        return u

    def u_gT():
        tk = None
        for i in range(NSEG):
            n = min(1024, TOWN - i * 1024)
            tk = v.op("tensor_tensor", waits=(Z["tk_hT"], Z["tk_wbce"]),
                      out=gT_s[:, i * 1024:i * 1024 + n],
                      in0=PS[2 + i][:, 0:n],
                      in1=wbc_s[:, i * 1024:i * 1024 + n], op=ALU.mult)
            Z["last_lo"][i].append(tk)
        Z["tk_gT"] = tk
        Z["tk_BTs"] = g.dma(BT_s[:, 0:D], BT_park.ap()[:, :], "g_bts",
                            waits=(Z["tk_BT"], Z["tk_hT"]))

    def mk_lora(o8):
        def u():
            for oc in range(o8 * 8, o8 * 8 + 8):
                lpar = oc % 2
                tk_le = None
                for seg in range(NSEG):
                    pq = PS[2 + seg]
                    tk2 = None
                    for c2 in range(2):
                        tk2 = t.op("matmul", pq[:, c2 * 512:c2 * 512 + 512],
                                   waits=((Z["tk_BTs"], Z["tk_gT"]) +
                                          tuple(Z["last_lo"][seg]))
                                         if c2 == 0 else (),
                                   produce=(c2 == 1),
                                   lhsT=BT_s[:, oc * 128:(oc + 1) * 128],
                                   rhs=gT_s[:, seg * 1024 + c2 * 512:
                                            seg * 1024 + c2 * 512 + 512],
                                   start=True, stop=True)
                    Z["last_lo"][seg][:] = []
                    tk_le = a.op("activation",
                                 waits=(tk2,) + tuple(Z["last_lev"][lpar]),
                                 out=lora_ev[:, lpar, :],
                                 in_=pq[:, 0:1024], func=AF.Copy)
                    Z["last_lo"][seg].append(tk_le)
                    Z["last_lev"][lpar][:] = []
                    tk_lp = a.dma(LORA_park.ap()[oc * 128:(oc + 1) * 128,
                                                 seg * 1024:seg * 1024 + 1024],
                                  lora_ev[:, lpar, :], "a_lp%d" % lpar,
                                  waits=(tk_le,))
                    Z["last_lev"][lpar].append(tk_lp)
        return u

    Zb = {"last_bwt": [[], []], "last_f_ps": [[], []], "last_fev": [[], []],
          "tk_bt": [None, None], "prev_bt": None}

    def mk_base(oc, seg):
        def u():
            par = oc % 2
            if seg == 0:
                tk_bw = s.dma(bw_row[:], bW_b.ap()[oc * 128:(oc + 1) * 128, :],
                              "s_bw%d" % par,
                              waits=tuple(Zb["last_bwt"][par]) + (tk_bW_b,) +
                                    ((Zb["prev_bt"],) if Zb["prev_bt"] else ()))
                tk_bt = s.dma_tr(bWT_t[:, par, :, :],
                                 bw_row.rearrange("p (c f) -> p c f", f=128),
                                 "s_bw%d" % par, waits=(tk_bw,))
                Zb["prev_bt"] = tk_bt
                Zb["tk_bt"][par] = tk_bt
                Zb["last_bwt"][par][:] = []
            q = (2 * oc + seg) % 2
            tkmm = None
            for dc in range(DC):
                for c2 in range(2):
                    c = seg * 1024 + c2 * 512
                    tkmm = t.op("matmul", PS[q][:, c2 * 512:c2 * 512 + 512],
                                waits=((Zb["tk_bt"][par],) +
                                       tuple(Zb["last_f_ps"][q]))
                                      if (dc == 0 and c2 == 0) else (),
                                produce=(dc == DC - 1 and c2 == 1),
                                lhsT=bWT_t[:, par, dc, :],
                                rhs=xnT[c // QTR][:, dc, c % QTR: c % QTR + 512],
                                start=(dc == 0), stop=(dc == DC - 1))
            Zb["last_f_ps"][q][:] = []
            if seg == 1:
                Zb["last_bwt"][par].append(tkmm)
            tk_fe = v.op("tensor_scalar", waits=(tkmm,) + tuple(Zb["last_fev"][q]),
                         out=ffn_ev[:, q, :],
                         in0=PS[q][:, 0:1024],
                         scalar1=bb_cols[:, oc:oc + 1], scalar2=None, op0=ALU.add)
            Zb["last_f_ps"][q].append(tk_fe)
            Zb["last_fev"][q][:] = []
            tk_fp = a.dma(FFN_park.ap()[oc * 128:(oc + 1) * 128,
                                        seg * 1024:seg * 1024 + 1024],
                          ffn_ev[:, q, :], "a_ffn%d" % q, waits=(tk_fe,))
            Zb["last_fev"][q].append(tk_fp)
        return u

    runits = [f for tc in range(NT) for f in (mk_tc(tc), mk_tc_b(tc))] + \
             [mk_bt(e, oc0) for e in range(E) for oc0 in range(0, DC, 8)] + \
             [u_wT, u_wbc, mk_hT(0, DC // 2), mk_hT(DC // 2, DC), u_gT] + \
             [mk_lora(o8) for o8 in range(DC // 8)]
    bunits = [mk_base(oc, seg) for oc in range(DC) for seg in range(NSEG)]
    ri = 0
    for bi, bu in enumerate(bunits):
        bu()
        for _ in range(2 if bi < 8 else 1):
            if ri < len(runits):
                runits[ri]()
                ri += 1
    while ri < len(runits):
        runits[ri]()
        ri += 1

    rec.barrier()

    last_h_ld = [[], []]
    last_ao = [[], []]
    last_pT = [[] for _ in range(8)]
    last_sT = [[], [], [], []]
    last_ot = [[], []]
    last_ivb = []
    last_mg = [[], []]
    last_tl = []
    qh_glob = 0
    pending_tail = []
    park_AO = []
    SLOT = lambda i: PSALL.ap()[:, i * 512:(i + 1) * 512]
    OT = lambda i: PSALL.ap()[:, 2048 + i * 512: 2048 + (i + 1) * 512]
    TL = PSALL.ap()[:, 3072:3584]
    TLA = lambda r: PSALL.ap()[r * 32:r * 32 + 1, 3584:3584 + QH]
    tla_guard = {0: None, 1: None}
    for h in range(H):
        par = h % 2
        sem_h = "s_kv%d" % par
        s.dma(Kh_sb[:, par, :], K_park.ap()[h, :, :], sem_h,
              waits=tuple(last_h_ld[par]))
        s.dma(Qh_sb[:, par, :], Q_park.ap()[h, :, :], sem_h)
        tk_ld = s.dma_tr(Vh_sb[:, par, :, :],
                         VT_park.ap()[h, :, :].rearrange("p (c f) -> p c f", f=128),
                         sem_h)
        last_h_ld[par] = []
        head_toks = []
        tk_pv_last = None
        for qh in range(NQH):
            oi = qh_glob % 2
            ot_cur, ot_nxt = OT(oi), OT(1 - oi)
            qh_glob += 1
            if pending_tail:
                pending_tail.pop(0)()
            npairs = (qh + 1) * QH // 128
            cps = [c for c in range(npairs) if c * 128 >= qh * QH] + \
                  [c for c in range(npairs) if c * 128 < qh * QH]
            firstcp, lastcp = cps[0], cps[-1]
            tk_exp = {}
            first = True
            pend = None
            tk_pv_con = None
            tla_st = {"tk": None}

            def emit_pv(p):
                nonlocal first, tk_pv_con
                ptr_, qs_, tk_m_ = p
                cpx = cpv[ptr_]
                for kind in (0, 1):
                    sl_i = ptr_ * 2 + kind
                    koff = kind * TOWN + cpx * 128
                    wts = (tk_m_,)
                    if first:
                        wts = wts + tuple(last_ot[oi]) + (tk_ld,)
                    tk = t.op("matmul", ot_cur[:, qs_:QH],
                              waits=wts,
                              produce=(kind == 1),
                              lhsT=Vh_sb[:, par, koff // 128, :],
                              rhs=pT_sb[:, sl_i, qs_:QH],
                              start=first,
                              stop=(cpx == lastcp and kind == 1))
                    first = False
                tk_pv_con = tk
                for kind in (0, 1):
                    gw = ()
                    if cpx == firstcp and kind == 0 and tla_guard[oi] is not None:
                        gw = (tla_guard[oi],)
                    tk_t = t.op("matmul", TLA(oi)[:, qs_:QH], waits=gw,
                                produce=(kind == 1),
                                lhsT=ones_col[:],
                                rhs=pT_sb[:, ptr_ * 2 + kind, qs_:QH],
                                start=(cpx == firstcp and kind == 0),
                                stop=(cpx == lastcp and kind == 1))
                tla_st["tk"] = tk_t
                last_pT[ptr_ * 2].append(tk_t)
                last_pT[ptr_ * 2 + 1].append(tk_t)

            cpv = {}
            qsv = {}
            for pi_, cp in enumerate(cps):
                qs = max(0, cp * 128 - qh * QH)
                prng = pi_ % 2
                ptr = pi_ % 4
                cpv[ptr] = cp
                qsv[ptr] = qs
                for kind in (0, 1):
                    sl_i = prng * 2 + kind
                    koff = kind * TOWN + cp * 128
                    tk_s = t.op("matmul", SLOT(sl_i)[:, qs:QH],
                                waits=(tk_ld,) + tuple(last_sT[sl_i]),
                                produce=(kind == 1),
                                lhsT=Kh_sb[:, par, koff:koff + 128],
                                rhs=Qh_sb[:, par, qh * QH + qs: (qh + 1) * QH],
                                start=True, stop=True)
                    last_sT[sl_i] = []
                pairview = PSALL.ap()[:, prng * 1024:(prng + 1) * 1024] \
                    .rearrange("p (a q) -> p a q", a=2)[:, :, qs:QH]
                tk_xp = a.op("activation",
                             waits=(tk_s, tk_negC) +
                                   tuple(last_pT[ptr * 2] + last_pT[ptr * 2 + 1]),
                             out=pT_sb[:, ptr * 2:ptr * 2 + 2, qs:QH],
                             in_=pairview,
                             func=AF.Exp, scale=ATT_SCALE, bias=negc_col[:])
                last_pT[ptr * 2] = []
                last_pT[ptr * 2 + 1] = []
                last_sT[prng * 2].append(tk_xp)
                last_sT[prng * 2 + 1].append(tk_xp)
                if cp * 128 >= qh * QH:
                    tk_m = v.op("tensor_tensor", waits=(tk_xp, tk_mp2, tk_mpa),
                                out=pT_sb[:, ptr * 2:ptr * 2 + 2, qs:qs + 128],
                                in0=pT_sb[:, ptr * 2:ptr * 2 + 2, qs:qs + 128],
                                in1=mp2[:], op=ALU.mult)
                else:
                    tk_m = tk_xp
                if pi_ == 2 and pending_tail:
                    pending_tail.pop(0)()
                if pend is not None:
                    emit_pv(pend)
                pend = (ptr, qs, tk_m)
            emit_pv(pend)
            tk_pv_last = tk_pv_con
            last_ot[oi] = []

            def make_tail(ot_cur=ot_cur, ot_nxt=ot_nxt, oi=oi, qh=qh, par=par,
                          tk_tla=tla_st["tk"], tk_pv_last=tk_pv_last,
                          my_toks=head_toks):
                state = {}

                def part1():
                    tk_ln = a.op("activation", waits=(tk_tla,),
                                 out=invd_s[:, 0:QH],
                                 in_=TLA(oi)[:, 0:QH], func=AF.Ln)
                    tla_guard[oi] = tk_ln
                    state["iv"] = a.op("activation", out=invd_s[:, 0:QH],
                                       in_=invd_s[:, 0:QH], func=AF.Exp, scale=-1.0)

                def part2():
                    nonlocal last_ivb, last_tl
                    tk_bc = t.op("matmul", TL[:, 0:QH],
                                 waits=(state["iv"],) + tuple(last_tl),
                                 produce=True,
                                 lhsT=ones_r128[:], rhs=invd_s[:, 0:QH],
                                 start=True, stop=True)
                    tk_cp = v.op("tensor_copy", waits=(tk_bc,) + tuple(last_ivb),
                                 out=ivb_s[:, 0:QH], in_=TL[:, 0:QH])
                    last_tl = [tk_cp]
                    tk_no = v.op("tensor_tensor",
                                 waits=(tk_pv_last, tk_cp) + tuple(last_ao[par]),
                                 out=ao_sb[:, par, qh * QH:(qh + 1) * QH],
                                 in0=ot_cur[:, 0:QH], in1=ivb_s[:, 0:QH],
                                 op=ALU.mult)
                    last_ivb = [tk_no]
                    last_ao[par] = []
                    last_ot[oi].append(tk_no)
                    my_toks.append(tk_no)
                return [part1, part2]
            pending_tail.extend(make_tail())
        while pending_tail:
            pending_tail.pop(0)()
        tk_pk = a.dma(AO_park.ap()[h, :, :], ao_sb[:, par, :],
                      "a_ao%d" % par, waits=tuple(head_toks))
        last_ao[par] = [tk_pk]
        last_h_ld[par] = [tk_pv_last, tk_pk] + \
                         [tok for l in last_sT for tok in l]
        park_AO.append(tk_pk)
        if h < DC:
            ml = h % 2
            tk_f1 = s.dma(ffn_ld[:, ml, :], FFN_park.ap()[h * 128:(h + 1) * 128, :],
                          "s_mg%d" % ml, waits=tuple(last_mg[ml]))
            tk_f2 = s.dma(lora_ld[:, ml, :], LORA_park.ap()[h * 128:(h + 1) * 128, :],
                          "s_mg%d" % ml)
            tk_ma = v.op("tensor_tensor", waits=(tk_f2,),
                         out=ffn_ld[:, ml, :], in0=ffn_ld[:, ml, :],
                         in1=lora_ld[:, ml, :], op=ALU.add)
            tk_mb = a.dma(FFN_park.ap()[h * 128:(h + 1) * 128, :],
                          ffn_ld[:, ml, :], "a_mg%d" % ml, waits=(tk_ma,))
            last_mg[ml] = [tk_ma, tk_mb]

    rec.barrier()

    last_at = [[], []]
    last_x4 = [[], []]
    last_f4 = [[], []]
    last_st = [[], []]
    last_p4ps = [[], []]
    prev_oh_mm = None
    n_store = 0
    for oh in range(NOH):
        tk_wo = None
        for dc in range(DC):
            tk_wo = g.dma(woh_sb[:, dc, :],
                          wo_in[dc * 128:(dc + 1) * 128, oh * OHALF:(oh + 1) * OHALF],
                          "g_wo",
                          waits=(prev_oh_mm,) if (dc == 0 and prev_oh_mm) else ())
        for tt in range(NT):
            par = tt % 2
            pspair = [PS[0], PS[1]] if par == 0 else [PS[2], PS[3]]
            sem_t = "s_p4%d" % par
            nspl = max(1, H // 8)
            for sp in range(nspl):
                h0, h1 = sp * (H // nspl), (sp + 1) * (H // nspl)
                s.dma(atT_sb[:, par, h0:h1, :],
                      AO_park.ap()[h0:h1, :, tt * 128:(tt + 1) * 128]
                           .rearrange("h p f -> p h f"), sem_t,
                      waits=tuple(last_at[par]) + tuple(last_f4[par]) +
                            tuple(last_x4[par]))
            last_at[par] = []
            s.dma_tr(ffn_p4[:, par, :],
                     FFN_park.ap()[oh * OHALF:(oh + 1) * OHALF,
                                   tt * 128:(tt + 1) * 128], sem_t)
            last_f4[par] = []
            tk_x4 = s.dma(x_p4[:, par, :],
                          x_in[tt * 128:(tt + 1) * 128, oh * OHALF:(oh + 1) * OHALF],
                          sem_t)
            last_x4[par] = []
            tk_at = tk_f4 = tk_x4
            tkmm = None
            for dc in range(DC):
                for off in range(0, OHALF, 512):
                    n = min(512, OHALF - off)
                    tkmm = t.op("matmul",
                                pspair[off // 1024][:, off % 1024: off % 1024 + n],
                                waits=((tk_at, tk_wo) + tuple(last_p4ps[par]))
                                      if (dc == 0 and off == 0) else (),
                                produce=(dc == DC - 1 and off + n >= OHALF),
                                lhsT=atT_sb[:, par, dc, :],
                                rhs=woh_sb[:, dc, off:off + n],
                                start=(dc == 0), stop=(dc == DC - 1))
            last_p4ps[par] = []
            last_at[par].append(tkmm)
            prev_oh_mm = tkmm
            tk_a2 = None
            for i in range((OHALF + 1023) // 1024):
                n = min(1024, OHALF - i * 1024)
                sl = slice(i * 1024, i * 1024 + n)
                tk_a1 = v.op("tensor_tensor",
                             waits=(tkmm, tk_f4) + tuple(last_st[par]),
                             out=out_p4[:, par, sl], in0=pspair[i][:, 0:n],
                             in1=ffn_p4[:, par, sl], op=ALU.add)
                last_p4ps[par].append(tk_a1)
                tk_a2 = v.op("tensor_tensor", waits=(tk_x4,),
                             out=out_p4[:, par, sl], in0=out_p4[:, par, sl],
                             in1=x_p4[:, par, sl], op=ALU.add)
            last_st[par] = []
            last_x4[par].append(tk_a2)
            last_f4[par].append(tk_a2)
            tk_st = a.dma(out_d[tt * 128:(tt + 1) * 128,
                                oh * OHALF:(oh + 1) * OHALF],
                          out_p4[:, par, :], "st%d" % par, waits=(tk_a2,))
            last_st[par] = [tk_st]
            n_store += 1
    s.op("nop", waits=tuple(rec.queue_totals()))

    rec.replay()
    return nc



_PROG_CACHE = {}

def _get_prog():
    if "full" not in _PROG_CACHE:
        _PROG_CACHE["full"] = build_program(CFG_FULL)
    return _PROG_CACHE["full"]


def make_in_maps(cfg, x, wq, wk, wv, wo, base_W, base_b, router_W, router_b,
                 A, B, expert_map, n_cores=8):
    D, S = cfg["D"], cfg["S"]
    NR, E, R = cfg["NR"], cfg["E"], cfg["R"]
    shared = {
        "wq": np.ascontiguousarray(wq, np.float32),
        "wk": np.ascontiguousarray(wk, np.float32),
        "wv": np.ascontiguousarray(wv, np.float32),
        "wo": np.ascontiguousarray(wo, np.float32),
        "base_W": np.ascontiguousarray(base_W, np.float32),
        "base_b": np.ascontiguousarray(base_b, np.float32).reshape(D // 128, 128),
        "router_W": np.ascontiguousarray(router_W, np.float32),
        "router_b": np.ascontiguousarray(router_b, np.float32).reshape(1, NR),
        "A_flat": np.ascontiguousarray(A, np.float32).reshape(E * R, D),
        "B_w": np.ascontiguousarray(B, np.float32).reshape(E * D, R),
        "expert_map": np.ascontiguousarray(expert_map, np.int32).reshape(NR, 1),
    }
    in_maps, orders = [], []
    for c in range(n_cores):
        bc, pc = c // 2, c % 2
        order = [2 * l + pc for l in range(S // 256)] + \
                [2 * l + (1 - pc) for l in range(S // 256)]
        orders.append(order)
        xb = np.ascontiguousarray(
            x[bc].reshape(S // 128, 128, D)[order].reshape(S, D), np.float32)
        pos = (np.asarray(order)[:, None] * 128 +
               np.arange(128)[None, :]).reshape(1, S).astype(np.float32)
        mpart = np.full((128, 128), 0.0 if pc == 0 else 1.0,
                        dtype=ml_dtypes.bfloat16)
        m = dict(shared)
        m["x_perm"] = xb
        m["pos_perm"] = pos
        m["m_part"] = mpart
        in_maps.append(m)
    return in_maps, orders


def assemble_out(cfg, results, orders, Bn):
    D, S = cfg["D"], cfg["S"]
    TOWN = S // 2
    out = np.empty((Bn, S, D), dtype=np.float32)
    for c, r in enumerate(results):
        bc = c // 2
        shard = np.asarray(r["out_own"]).reshape(TOWN // 128, 128, D)
        for l in range(TOWN // 128):
            gt = orders[c][l]
            out[bc, gt * 128:(gt + 1) * 128, :] = shard[l]
    return out


def kernel(L0_attn_hidden_states, L0_attn_wq, L0_attn_wk, L0_attn_wv, L0_attn_wo,
           L0_attn_kq1, L0_attn_kq2, L0_attn_kk1, L0_attn_kk2, L0_attn_scalingD,
           L0_attn_interaction_k, L0_attn_scalingD2, L0_ffn_base_W, L0_ffn_base_b,
           L0_ffn_router_W, L0_ffn_router_b, L0_ffn_A, L0_ffn_B, L0_ffn_expert_map):
    cfg = CFG_FULL
    x = np.ascontiguousarray(L0_attn_hidden_states, dtype=np.float32)
    nc = _get_prog()
    in_maps, orders = make_in_maps(
        cfg, x, L0_attn_wq, L0_attn_wk, L0_attn_wv, L0_attn_wo,
        L0_ffn_base_W, L0_ffn_base_b, L0_ffn_router_W, L0_ffn_router_b,
        L0_ffn_A, L0_ffn_B, L0_ffn_expert_map)
    res = run_bass_kernel_spmd(nc, in_maps, core_ids=list(range(8)))
    return assemble_out(cfg, res.results, orders, x.shape[0])


# version 38
# speedup vs baseline: 1.0882x; 1.0035x over previous
import numpy as np
import ml_dtypes
import concourse.bass as bass
import concourse.mybir as mybir
from concourse.bass_utils import run_bass_kernel_spmd

F32 = mybir.dt.float32
BF16 = mybir.dt.bfloat16
I32 = mybir.dt.int32
AF = mybir.ActivationFunctionType
ALU = mybir.AluOpType
AX = mybir.AxisListType

MAGIC = float(np.float32(12582912.0))
TWO_PI = float(np.float32(2 * np.pi))
INV_2PI = float(np.float32(1.0 / (2 * np.pi)))
PI_CLAMP = 3.14159203
HALF_PI = float(np.float32(np.pi / 2))
EXP_SHIFT = 16.0
LN_EPS = 1e-5

CFG_FULL = dict(D=4096, S=4096, H=32, NR=15, E=8, R=16)



class _Eng:
    def __init__(self, rec, name, chain_sem, chained):
        self._rec, self._name = rec, name
        self._chain, self._chained = chain_sem, chained
        self._n = 0
        self._waited = {}
        self.items = []

    def wait(self, token):
        if token is None:
            return
        sem, val = token
        key = id(sem)
        if self._waited.get(key, -1) >= val:
            return
        self._waited[key] = val
        self.items.append(("wait", sem, val))

    def op(self, meth, *args, waits=(), produce=False, inc=None, **kw):
        for tk in waits:
            self.wait(tk)
        if self._chained and self._n:
            self.wait((self._chain, self._n))
        tok, incs = None, None
        if inc is not None:
            sem, by = inc
            c = self._rec.sem_counts
            c[id(sem)] = c.get(id(sem), 0) + by
            tok, incs = (sem, c[id(sem)]), (sem, by)
        elif self._chained or produce:
            self._n += 1
            tok, incs = (self._chain, self._n), (self._chain, 1)
        self.items.append(("call", meth, args, kw, incs))
        return tok

    def dma(self, out, in_, sem, waits=(), **kw):
        h = self._rec.dsem(sem)
        return self.op("dma_start", waits=waits, inc=(h, 16), out=out, in_=in_, **kw)

    def dma_tr(self, out, in_, sem, waits=()):
        h = self._rec.dsem(sem)
        return self.op("dma_start_transpose", out, in_, waits=waits, inc=(h, 16))


class Rec:
    def __init__(self, nc):
        self.nc = nc
        self.sem_counts = {}
        self.dsems = {}
        self.t = _Eng(self, "tensor", nc.alloc_semaphore("ch_pe"), chained=False)
        self.v = _Eng(self, "vector", nc.alloc_semaphore("ch_dve"), chained=True)
        self.a = _Eng(self, "scalar", nc.alloc_semaphore("ch_act"), chained=True)
        self.g = _Eng(self, "gpsimd", nc.alloc_semaphore("ch_pool"), chained=True)
        self.s = _Eng(self, "sync", nc.alloc_semaphore("ch_sp"), chained=False)
        self.engs = {"tensor": self.t, "vector": self.v, "scalar": self.a,
                     "gpsimd": self.g, "sync": self.s}

    def dsem(self, name):
        if name not in self.dsems:
            self.dsems[name] = self.nc.alloc_semaphore("dq_" + name)
        return self.dsems[name]

    def queue_totals(self):
        return [(h, self.sem_counts.get(id(h), 0)) for h in self.dsems.values()
                if self.sem_counts.get(id(h), 0) > 0]

    def barrier(self, tokens=()):
        alltk = list(tokens) + self.queue_totals() + \
                [(e._chain, e._n) for e in self.engs.values() if e._n > 0]
        for eng in self.engs.values():
            for tk in alltk:
                eng.wait(tk)

    def replay(self):
        with self.nc.Block() as block:
            for name, eng in self.engs.items():
                def make(items):
                    def run(e):
                        for it in items:
                            if it[0] == "wait":
                                e.wait_ge(it[1], it[2])
                            else:
                                _, meth, args, kw, incs = it
                                r = getattr(e, meth)(*args, **kw)
                                if incs is not None:
                                    r.then_inc(incs[0], incs[1])
                    return run
                getattr(block, name)(make(eng.items))



def build_program(cfg):
    D, S, H, NR, E, R = (cfg[k] for k in ("D", "S", "H", "NR", "E", "R"))
    HD = 128
    assert D == H * HD
    DC = D // 128
    TOWN = S // 2
    NT = TOWN // 128
    NTB = S // 128
    QTR = TOWN // 2
    NQT = QTR // 128
    assert QTR % 512 == 0, "psum zero regions are bank-granular"

    ER = E * R
    assert ER == 128
    VW = min(512, D)
    NVQ = D // VW
    QH = min(512, TOWN)
    NQH = TOWN // QH
    NSEG = (TOWN + 1023) // 1024
    OHALF = min(cfg.get('OHALF', 2048), D)
    NOH = D // OHALF
    ATT_SCALE = float(1.0 / np.sqrt(HD))

    nc = bass.Bass("TRN2", target_bir_lowering=False)

    x_in = nc.dram_tensor("x_perm", [S, D], F32, kind="ExternalInput").ap()
    pos_in = nc.dram_tensor("pos_perm", [1, S], F32, kind="ExternalInput").ap()
    mpart_in = nc.dram_tensor("m_part", [128, 128], BF16, kind="ExternalInput").ap()
    wq_in = nc.dram_tensor("wq", [D, D], F32, kind="ExternalInput").ap()
    wk_in = nc.dram_tensor("wk", [D, D], F32, kind="ExternalInput").ap()
    wv_in = nc.dram_tensor("wv", [D, D], F32, kind="ExternalInput").ap()
    wo_in = nc.dram_tensor("wo", [D, D], F32, kind="ExternalInput").ap()
    bW_in = nc.dram_tensor("base_W", [D, D], F32, kind="ExternalInput").ap()
    bb_in = nc.dram_tensor("base_b", [DC, 128], F32, kind="ExternalInput").ap()
    rW_in = nc.dram_tensor("router_W", [NR, D], F32, kind="ExternalInput").ap()
    rb_in = nc.dram_tensor("router_b", [1, NR], F32, kind="ExternalInput").ap()
    A_in = nc.dram_tensor("A_flat", [ER, D], F32, kind="ExternalInput").ap()
    B_in = nc.dram_tensor("B_w", [E * D, R], F32, kind="ExternalInput").ap()
    emap_in = nc.dram_tensor("expert_map", [NR, 1], I32, kind="ExternalInput").ap()
    out_d = nc.dram_tensor("out_own", [TOWN, D], F32, kind="ExternalOutput").ap()

    ident_bf_d = nc.inline_tensor(np.eye(128, dtype=ml_dtypes.bfloat16), "c_identb").ap()
    ident_f_d = nc.inline_tensor(np.eye(128, dtype=np.float32), "c_identf").ap()
    invf_np = (10000.0 ** (-(np.arange(128) % 64) * 2.0 / HD)).astype(np.float32)
    invf_d = nc.inline_tensor(invf_np.reshape(1, 128), "c_invf").ap()
    sgn_d = nc.inline_tensor(
        np.where(np.arange(128) < 64, 1.0, -1.0).astype(np.float32).reshape(128, 1),
        "c_sgn").ap()
    diag_d = nc.inline_tensor(
        (np.arange(128)[None, :] >= np.arange(128)[:, None]).astype(ml_dtypes.bfloat16),
        "c_diag").ap()
    exp16_d = nc.inline_tensor(
        (np.arange(ER)[None, :] // R == np.arange(E)[:, None]).astype(ml_dtypes.bfloat16),
        "c_exp16").ap()
    eiota_d = nc.inline_tensor(
        np.tile(np.arange(E, dtype=np.float32), (NR, 1)), "c_eiota").ap()

    wq_t = nc.dram_tensor("wq_t", [H, 128, DC, 128], BF16, kind="Internal")
    wk_t = nc.dram_tensor("wk_t", [H, 128, DC, 128], BF16, kind="Internal")
    wv_t = nc.dram_tensor("wv_t", [H, 128, DC, 128], BF16, kind="Internal")
    bW_b = nc.dram_tensor("bW_b", [D, D], BF16, kind="Internal")
    B_b = nc.dram_tensor("B_b", [E * D, R], BF16, kind="Internal")
    BT_park = nc.dram_tensor("BT_park", [ER, D], BF16, kind="Internal")
    K_park = nc.dram_tensor("K_park", [H, 128, S], BF16, kind="Internal")
    Q_park = nc.dram_tensor("Q_park", [H, 128, TOWN], BF16, kind="Internal")
    VT_park = nc.dram_tensor("VT_park", [H, 128, S], BF16, kind="Internal")
    FFN_park = nc.dram_tensor("FFN_park", [D, TOWN], BF16, kind="Internal")
    LORA_park = nc.dram_tensor("LORA_park", [D, TOWN], BF16, kind="Internal")
    A_db = nc.dram_tensor("A_db", [ER, D], BF16, kind="Internal")
    AO_park = nc.dram_tensor("AO_park", [H, 128, TOWN], BF16, kind="Internal")

    rec = Rec(nc)
    t, v, a, g, s = rec.t, rec.v, rec.a, rec.g, rec.s

    PSALL = nc.alloc_psum_tensor("psall", [128, 4096], F32)
    PS = [PSALL.ap()[:, i * 1024:(i + 1) * 1024] for i in range(4)]
    psA, psB, psC, psD = PS

    def sbt(name, shape, dt=F32):
        return nc.alloc_sbuf_tensor(name, shape, dt)
    ident_bs = sbt("ident_bs", [128, 128], BF16)
    ident_fs = sbt("ident_fs", [128, 128], F32)
    invf_s = sbt("invf_s", [1, 128], F32)
    sgn_s = sbt("sgn_s", [128, 1], F32)
    diag_s = sbt("diag_s", [128, 128], BF16)
    mpart_s = sbt("mpart_s", [128, 128], BF16)
    mp2 = sbt("mp2", [128, 2, 128], BF16)
    exp16_s = sbt("exp16_s", [E, 128], BF16)
    eio_s = sbt("eio_s", [NR, E], F32)
    emap_i = sbt("emap_i", [NR, 1], I32)
    emap_f = sbt("emap_f", [NR, 1], F32)
    sel_f = sbt("sel_f", [NR, E], F32)
    sel_b = sbt("sel_b", [NR, E], BF16)
    ones_col = sbt("ones_col", [128, 1], BF16)
    negc_col = sbt("negc_col", [128, 1], F32)
    ones_r128 = sbt("ones_r128", [1, 128], F32)
    rb_row = sbt("rb_row", [1, NR], F32)
    bb_dc = sbt("bb_dc", [DC, 128], F32)
    bb_cols = sbt("bb_cols", [128, DC], F32)
    stats = sbt("stats", [128, 8], F32)
    posq = sbt("posq", [1, max(QTR, 128)], F32)
    lgs = sbt("lgs", [128, NR], F32)
    cur_s = sbt("cur_s", [128, NR], F32)
    eq_s = sbt("eq_s", [128, NR], F32)
    pi_b = sbt("pi_b", [128, NR], BF16)
    lcol = sbt("lcol", [128, 6], F32)
    Bt_in = sbt("Bt_in", [128, 8, R], BF16)
    Bt_ev = sbt("Bt_ev", [R, 8 * 128], BF16)
    rw_pad = sbt("rw_pad", [16, D], BF16)
    invd_s = sbt("invd_s", [1, 1024], F32)

    AR_BYTES = 189696
    arena = nc.alloc_sbuf_tensor("arena", [128, AR_BYTES // 2], BF16)

    class Alloc:
        def __init__(self, off=0):
            self.off = off
        def take(self, shape, dt=BF16):
            esz = mybir.dt.size(dt)
            per_p = int(np.prod(shape[1:])) * esz
            o = (self.off + 31) // 32 * 32
            self.off = o + per_p
            assert self.off <= AR_BYTES, f"arena overflow {self.off}"
            flat = arena.ap().bitcast(dt)[0:shape[0], o // esz: (o + per_p) // esz]
            if len(shape) == 2:
                return flat
            dims = {f"d{i}": shape[i] for i in range(1, len(shape))}
            pat = ("p (" + " ".join(dims) + ") -> p " + " ".join(dims))
            return flat.rearrange(pat, **dims)

    al = Alloc()
    xnT = [al.take([128, DC, QTR]), al.take([128, DC, QTR])]
    P2_BASE = al.off
    wg_sb = al.take([128, 2, DC, 128])
    xbuf = al.take([128, D], F32)
    xn_tok = al.take([128, D])
    cosq = al.take([128, QTR])
    sinq = al.take([128, QTR])
    qk_e = al.take([128, 2, QTR])
    th_t = al.take([128, QTR], F32)
    rot_t1 = th_t.bitcast(BF16)[:, 0:QTR]

    al2 = Alloc(P2_BASE)
    bWT_t = al2.take([128, 2, DC, 128])
    AT_s = al2.take([128, DC, 128])
    rwT_s = al2.take([128, DC, 16])
    BT_s = AT_s.rearrange("p a b -> p (a b)")
    piT = al2.take([NR, TOWN])
    wT_s = al2.take([E, TOWN])
    wbc_s = al2.take([128, TOWN])
    gT_s = al2.take([128, TOWN])
    ffn_ev = al2.take([128, 2, 1024])
    lora_ev = al2.take([128, 2, 1024])
    bw_row = al2.take([128, D])

    al3 = Alloc()
    Kh_sb = al3.take([128, 2, S])
    Vh_sb = al3.take([128, 2, NTB, 128])
    Qh_sb = al3.take([128, 2, TOWN])
    pT_sb = al3.take([128, 8, QH])
    ao_sb = al3.take([128, 2, TOWN])
    dacc_g = al3.take([128, QH], F32)
    dacc_v = al3.take([128, QH], F32)
    ivb_s = al3.take([128, QH], F32)
    ffn_ld = al3.take([128, 2, TOWN])
    lora_ld = al3.take([128, 2, TOWN])

    al4 = Alloc()
    woh_sb = al4.take([128, DC, OHALF])
    atT_sb = al4.take([128, 2, DC, 128])
    ffn_p4 = al4.take([128, 2, OHALF])
    x_p4 = al4.take([128, 2, OHALF], F32)
    out_p4 = al4.take([128, 2, OHALF], F32)

    for dst, srcp in [(ident_bs[:], ident_bf_d), (ident_fs[:], ident_f_d),
                      (invf_s[:], invf_d), (sgn_s[:], sgn_d), (diag_s[:], diag_d),
                      (mpart_s[:], mpart_in), (exp16_s[:], exp16_d),
                      (eio_s[:], eiota_d), (emap_i[:], emap_in),
                      (rb_row[:], rb_in), (bb_dc[:], bb_in)]:
        tk_consts = s.dma(dst, srcp, "const")
    tk_idb = tk_idf = tk_invf = tk_sgn = tk_diag = tk_mp = tk_consts
    tk_x16 = tk_eio = tk_emap = tk_rb = tk_bb2 = tk_consts
    tk_mpa = v.op("tensor_copy", waits=(tk_diag,), out=mp2[:, 0, :], in_=diag_s[:])
    tk_mp2 = v.op("tensor_copy", waits=(tk_mp,), out=mp2[:, 1, :], in_=mpart_s[:])
    v.op("memset", ones_col[:], 1.0)
    tk_negC = v.op("memset", negc_col[:], -EXP_SHIFT)
    tk_ones = v.op("memset", ones_r128[:], 1.0)

    def cast_tiled(dst, src, sem, first_solo=False):
        if first_solo:
            groups = [[0], [1, 2, 3]] + \
                     [list(range(hg, hg + 4)) for hg in range(4, H, 4)]
        else:
            groups = [list(range(hg, hg + 4)) for hg in range(0, H, 4)]
        gtoks = {}
        step = max(1, 8)
        for gi, hs in enumerate(groups):
            tk = None
            for h in hs:
                for dq in range(0, DC, step):
                    dn = min(step, DC - dq)
                    tk = g.dma(dst.ap()[h, :, dq:dq + dn, :],
                               src.rearrange("(dc p) (h j) -> h p dc j",
                                             p=128, j=128)
                                  [h, :, dq:dq + dn, :],
                               "%s_%d" % (sem, gi))
            for h in hs:
                gtoks[h] = tk
        return [gtoks[h] for h in range(H)]
    tks_wk = cast_tiled(wk_t, wk_in, "g_wk", first_solo=True)
    tks_wv = cast_tiled(wv_t, wv_in, "g_wv")
    tks_wq = cast_tiled(wq_t, wq_in, "g_wq")
    tk_bW_b = None
    for dc in range(DC):
        tk_bW_b = g.dma(bW_b.ap()[dc * 128:(dc + 1) * 128, :],
                        bW_in[dc * 128:(dc + 1) * 128, :], "g_bW")
    tk_B_b = g.dma(B_b.ap()[:, :], B_in, "g_B")

    tk_bbT = t.op("transpose", psA[0:128, 0:DC], produce=True,
                  waits=(tk_bb2, tk_idf),
                  in_=bb_dc[:], identity=ident_fs[0:DC, 0:DC])
    tk_bbc = v.op("tensor_copy", waits=(tk_bbT,), out=bb_cols[:],
                  in_=psA[0:128, 0:DC])

    v.op("tensor_copy", waits=(tk_emap,), out=emap_f[:], in_=emap_i[:])
    v.op("tensor_scalar", waits=(tk_eio,), out=sel_f[:], in0=eio_s[:],
         scalar1=emap_f[:], scalar2=None, op0=ALU.is_equal)
    tk_selb = v.op("tensor_copy", out=sel_b[:], in_=sel_f[:])

    last_xbuf = []
    last_xn = []
    last_stat = []
    last_xnT = [[], []]
    last_wg = [[], []]
    last_qk = [[], []]
    last_kq_ps = [[tk_bbc], []]
    last_tab = []
    last_thbuf = []
    last_rot = []
    last_posrd = []
    park_K, park_Q, park_V = [], [], []

    def ln_tile(qi, slot, tc):
        nonlocal last_xbuf, last_xn, last_stat
        buf = xnT[slot]
        row0 = qi * QTR + tc * 128
        tk_x = s.dma(xbuf[:], x_in[row0:row0 + 128, :], "s_x",
                     waits=tuple(last_xbuf))
        last_xbuf = []
        tk_sum = v.op("reduce_sum", waits=(tk_x,), out=stats[:, 0:1],
                      in_=xbuf[:], axis=AX.X)
        tk_sq = a.op("activation", waits=(tk_x,) + tuple(last_xn) + tuple(last_stat),
                     out=xn_tok[:],
                     in_=xbuf[:], func=AF.Square, accum_out=stats[:, 1:2])
        v.op("tensor_scalar", waits=(tk_sum,), out=stats[:, 2:3],
             in0=stats[:, 0:1], scalar1=float(1.0 / D), scalar2=None, op0=ALU.mult)
        v.op("tensor_tensor", out=stats[:, 3:4], in0=stats[:, 2:3],
             in1=stats[:, 2:3], op=ALU.mult)
        v.op("tensor_scalar", waits=(tk_sq,), out=stats[:, 4:5],
             in0=stats[:, 1:2], scalar1=float(1.0 / D), scalar2=None, op0=ALU.mult)
        v.op("tensor_tensor", out=stats[:, 4:5], in0=stats[:, 4:5],
             in1=stats[:, 3:4], op=ALU.subtract)
        tk_ve = v.op("tensor_scalar", out=stats[:, 5:6], in0=stats[:, 4:5],
                     scalar1=float(LN_EPS), scalar2=None, op0=ALU.add)
        last_stat = [tk_ve]
        a.op("activation", waits=(tk_ve,), out=stats[:, 6:7], in_=stats[:, 5:6],
             func=AF.Ln)
        tk_rs = a.op("activation", out=stats[:, 7:8], in_=stats[:, 6:7],
                     func=AF.Exp, scale=-0.5)
        tk_xn = v.op("tensor_scalar", waits=(tk_rs,), out=xn_tok[:], in0=xbuf[:],
                     scalar1=stats[:, 2:3], scalar2=stats[:, 7:8],
                     op0=ALU.subtract, op1=ALU.mult)
        last_xbuf = [tk_xn]
        tk_tr = a.dma_tr(buf[:, :, tc * 128:(tc + 1) * 128],
                         xn_tok.rearrange("p (c f) -> p c f", f=128),
                         "s_tr%d" % slot,
                         waits=tuple([tk_xn] + last_xnT[slot]))
        last_xnT[slot] = []
        last_xn = [tk_tr]
        return tk_tr

    def rotary_tables(qi):
        nonlocal last_tab, last_thbuf
        tk_p = s.dma(posq[:, 0:QTR], pos_in[:, qi * QTR:(qi + 1) * QTR], "s_pos",
                     waits=tuple(last_posrd))
        tk_th = None
        for off in range(0, QTR, 512):
            n = min(512, QTR - off)
            tk_th = t.op("matmul", psC[:, off:off + n],
                         waits=((tk_p, tk_invf) + tuple(last_tab)) if off == 0 else (),
                         produce=(off + n >= QTR),
                         lhsT=invf_s[:], rhs=posq[:, off:off + n],
                         start=True, stop=True)
        last_posrd[:] = [tk_th]
        last_tab = []
        sins = []
        for phase, dst in (("sin", sinq), ("cos", cosq)):
            shift = 0.0 if phase == "sin" else 0.25
            psh = 0.0 if phase == "sin" else HALF_PI
            v.op("tensor_scalar", waits=(tk_th,) + tuple(last_thbuf), out=th_t[:],
                 in0=psC[:, 0:QTR], scalar1=INV_2PI, scalar2=shift,
                 op0=ALU.mult, op1=ALU.add)
            last_thbuf = []
            v.op("tensor_scalar", out=th_t[:], in0=th_t[:],
                 scalar1=MAGIC, scalar2=MAGIC, op0=ALU.add, op1=ALU.subtract)
            v.op("tensor_scalar", out=th_t[:], in0=th_t[:],
                 scalar1=-TWO_PI, scalar2=None, op0=ALU.mult)
            tk2 = v.op("scalar_tensor_tensor", out=th_t[:], in0=psC[:, 0:QTR],
                       scalar=psh, in1=th_t[:], op0=ALU.add, op1=ALU.add)
            last_tab.append(tk2)
            tk_cl = v.op("tensor_scalar", out=th_t[:], in0=th_t[:],
                         scalar1=PI_CLAMP, scalar2=-PI_CLAMP, op0=ALU.min, op1=ALU.max)
            tk_sin = a.op("activation", out=dst[:, 0:QTR], in_=th_t[:], func=AF.Sin,
                          waits=(tk_cl,) + tuple(last_rot))
            sins.append(tk_sin)
            last_thbuf.append(tk_sin)
        tk_fold = v.op("tensor_scalar", waits=(sins[0], tk_sgn), out=sinq[:, 0:QTR],
                       in0=sinq[:, 0:QTR], scalar1=sgn_s[:], scalar2=None,
                       op0=ALU.mult)
        return (sins[1], tk_fold)

    def rotary_apply(sl, tk_src, tk_tab):
        w = (tk_src,) + tk_tab
        v.op("tensor_tensor", waits=w, out=rot_t1[0:64, 0:QTR],
             in0=sl[64:128, :], in1=sinq[64:128, 0:QTR], op=ALU.mult)
        v.op("tensor_tensor", out=rot_t1[64:128, 0:QTR],
             in0=sl[0:64, :], in1=sinq[0:64, 0:QTR], op=ALU.mult)
        v.op("tensor_tensor", out=sl, in0=sl, in1=cosq[:, 0:QTR], op=ALU.mult)
        tk = v.op("tensor_tensor", out=sl, in0=sl, in1=rot_t1[:, 0:QTR], op=ALU.add)
        last_rot[:] = [tk]
        return tk

    QORDER = [2, 3, 0, 1]
    ln_toks = {QORDER[0]: [ln_tile(QORDER[0], 0, tc) for tc in range(NQT)]}
    for oi, qi in enumerate(QORDER):
        slot = oi % 2
        tr_toks = ln_toks[qi]
        buf = xnT[slot]
        tk_tab = rotary_tables(qi)
        own = qi < 2
        nxt = QORDER[oi + 1] if oi + 1 < len(QORDER) else None
        ln_next = []
        nheads_tot = H * (2 if own else 1)
        ln_every = max(1, nheads_tot // NQT)
        hcount = 0

        def maybe_ln():
            nonlocal hcount
            hcount += 1
            if nxt is not None and hcount % ln_every == 0 and len(ln_next) < NQT:
                ln_next.append(ln_tile(nxt, (oi + 1) % 2, len(ln_next)))

        projs = [("K", wk_t, tks_wk, K_park, qi * QTR),
                 ("V", wv_t, tks_wv, VT_park, qi * QTR)]
        if own:
            projs.append(("Q", wq_t, tks_wq, Q_park, qi * QTR))
        for pname, w_t, wtoks, park, col0 in projs:
            pk_list = {"K": park_K, "Q": park_Q, "V": park_V}[pname]
            for h in range(H):
                pslot = h % 2
                pbuf = psA if pslot == 0 else psB
                tk_w = s.dma(wg_sb[:, pslot, :, :],
                             w_t.ap()[h, :, :, :], "s_wg%d" % pslot,
                             waits=tuple(last_wg[pslot]) + (wtoks[h],))
                last_wg[pslot] = []
                tkmm = None
                for dc in range(DC):
                    for off in range(0, QTR, 512):
                        n = min(512, QTR - off)
                        tkmm = t.op(
                            "matmul", pbuf[:, off:off + n],
                            waits=((tk_w, tr_toks[-1]) + tuple(last_kq_ps[pslot]))
                                  if (dc == 0 and off == 0) else (),
                            produce=(dc == DC - 1 and off + n >= QTR),
                            lhsT=wg_sb[:, pslot, dc, :],
                            rhs=buf[:, dc, off:off + n],
                            start=(dc == 0), stop=(dc == DC - 1))
                last_kq_ps[pslot] = []
                last_wg[pslot].append(tkmm)
                sl = qk_e[:, pslot, :]
                tk_ev = a.op("activation", waits=(tkmm,) + tuple(last_qk[pslot]),
                             out=sl, in_=pbuf[:, 0:QTR], func=AF.Copy)
                last_kq_ps[pslot].append(tk_ev)
                tk_ro = tk_ev if pname == "V" else rotary_apply(sl, tk_ev, tk_tab)
                tk_pk = a.dma(park.ap()[h, :, col0:col0 + QTR], sl,
                              "a_pk%d" % pslot, waits=(tk_ro,))
                last_qk[pslot] = [tk_pk]
                pk_list.append(tk_pk)
                maybe_ln()
        while nxt is not None and len(ln_next) < NQT:
            ln_next.append(ln_tile(nxt, (oi + 1) % 2, len(ln_next)))
        if nxt is not None:
            ln_toks[nxt] = ln_next

        last_xnT[slot] = [tkmm]

    rec.barrier()

    tk_A_db = g.dma(A_db.ap(), A_in, "g_A")
    tk_rwz = v.op("memset", rw_pad[:], 0.0)
    tk_rwl = g.dma(rw_pad[0:NR, :], rW_in, "g_rw", waits=(tk_rwz,))
    tk_rwT = s.dma_tr(rwT_s[:], rw_pad.ap().rearrange("p (c f) -> p c f", f=128),
                      "s_p2tr", waits=(tk_rwl,))
    tk_AT = s.dma_tr(AT_s[:], A_db.ap().rearrange("p (c f) -> p c f", f=128),
                     "s_p2at", waits=(tk_A_db,))

    last_r_ps = [[], []]
    Z = {"tk_pe": None, "tk_pi": None, "tk_pt_prev": None,
         "tk_BT": None, "tk_btr_prev": None, "tk_bev_prev": None,
         "tk_wT": None, "tk_wTe": None, "tk_wbc": None, "tk_wbce": None,
         "tk_hT": None, "tk_gT": None, "tk_BTs": None,
         "last_lo": [[], []], "last_lev": [[], []]}

    def mk_tc(tc):
        def u():
            pslot = tc % 2
            pbuf = PS[2] if pslot == 0 else PS[3]
            for dc in range(DC):
                bufq = xnT[(tc * 128) // QTR]
                t.op("matmul", pbuf[:, 0:NR],
                     waits=((tk_rwT,) + tuple(last_r_ps[pslot])) if dc == 0 else (),
                     lhsT=bufq[:, dc, (tc * 128) % QTR:(tc * 128) % QTR + 128],
                     rhs=rwT_s[:, dc, 0:NR],
                     start=(dc == 0), stop=False)
            last_r_ps[pslot][:] = []
            tkmm = t.op("matmul", pbuf[:, 0:NR], produce=True,
                        waits=(tk_rb, tk_ones),
                        lhsT=ones_r128[:], rhs=rb_row[:], start=False, stop=True)
            tk_c = v.op("tensor_copy", waits=(tkmm,), out=lgs[:], in_=pbuf[:, 0:NR])
            last_r_ps[pslot].append(tk_c)
            v.op("reduce_max", out=lcol[:, 0:1], in_=lgs[:], axis=AX.X)
            tk_nm = v.op("tensor_scalar", out=lcol[:, 1:2], in0=lcol[:, 0:1],
                         scalar1=-1.0, scalar2=None, op0=ALU.mult)
            tk_e = a.op("activation", waits=(tk_nm,), out=lgs[:], in_=lgs[:],
                        func=AF.Exp, bias=lcol[:, 1:2], accum_out=lcol[:, 2:3])
            v.op("reciprocal", waits=(tk_e,), out=lcol[:, 3:4], in_=lcol[:, 2:3])
            v.op("tensor_scalar", out=lgs[:], in0=lgs[:],
                 scalar1=lcol[:, 3:4], scalar2=None, op0=ALU.mult)
            v.op("tensor_copy", out=cur_s[:], in_=lgs[:])
            for _ in range(E - 1):
                v.op("reduce_max", out=lcol[:, 4:5], in_=cur_s[:], axis=AX.X)
                v.op("tensor_scalar", out=eq_s[:], in0=cur_s[:],
                     scalar1=lcol[:, 4:5], scalar2=None, op0=ALU.is_equal)
                v.op("scalar_tensor_tensor", out=cur_s[:], in0=eq_s[:],
                     scalar=-2.0, in1=cur_s[:], op0=ALU.mult, op1=ALU.add)
            v.op("reduce_max", out=lcol[:, 5:6], in_=cur_s[:], axis=AX.X)
            v.op("tensor_scalar", out=eq_s[:], in0=lgs[:],
                 scalar1=lcol[:, 5:6], scalar2=None, op0=ALU.is_ge)
            tk_pi = v.op("tensor_tensor",
                         waits=(Z["tk_pt_prev"],) if Z["tk_pt_prev"] else (),
                         out=pi_b[:], in0=lgs[:], in1=eq_s[:],
                         op=ALU.mult)
            Z["tk_pi"] = tk_pi
        return u

    def mk_tc_b(tc):
        def u():
            pslot = tc % 2
            pbuf = PS[2] if pslot == 0 else PS[3]
            tk_pt = t.op("transpose", pbuf.bitcast(BF16)[0:NR, 1024:1024 + 128],
                         produce=True, waits=(Z["tk_pi"], tk_idb),
                         in_=pi_b[:], identity=ident_bs[:])
            Z["tk_pt_prev"] = tk_pt
            tk_pe = a.op("activation", waits=(tk_pt,),
                         out=piT[:, tc * 128:(tc + 1) * 128],
                         in_=pbuf.bitcast(BF16)[0:NR, 1024:1024 + 128],
                         func=AF.Copy)
            Z["tk_pe"] = tk_pe
            last_r_ps[pslot].append(tk_pe)
        return u

    def mk_bt(e, oc0):
        def u():
            nb = min(8, DC - oc0)
            tk_bl = g.dma(Bt_in[:, 0:nb, :],
                          B_b.ap()[e * D + oc0 * 128: e * D + (oc0 + nb) * 128, :]
                             .rearrange("(c p) r -> p c r", p=128),
                          "g_bt",
                          waits=(tk_B_b,) +
                                ((Z["tk_btr_prev"],) if Z["tk_btr_prev"] else ()))
            tk_btr = None
            for i in range(nb):
                tk_btr = t.op("transpose",
                              PS[2].bitcast(BF16)[0:R, 1024 + i * 128:1152 + i * 128],
                              produce=(i == nb - 1),
                              waits=((tk_bl, tk_idb) + tuple(last_r_ps[0]) +
                                     ((Z["tk_bev_prev"],) if Z["tk_bev_prev"] else ()))
                                    if i == 0 else (),
                              in_=Bt_in[:, i, :], identity=ident_bs[:])
            Z["tk_btr_prev"] = tk_btr
            last_r_ps[0][:] = []
            tk_bev = a.op("activation",
                          waits=(tk_btr,) + ((Z["tk_BT"],) if Z["tk_BT"] else ()),
                          out=Bt_ev[:, 0:nb * 128],
                          in_=PS[2].bitcast(BF16)[0:R, 1024:1024 + nb * 128],
                          func=AF.Copy, scale=2.0)
            Z["tk_bev_prev"] = tk_bev
            last_r_ps[0].append(tk_bev)
            Z["tk_BT"] = g.dma(BT_park.ap()[e * R:(e + 1) * R,
                                            oc0 * 128:(oc0 + nb) * 128],
                               Bt_ev[:, 0:nb * 128], "g_btp", waits=(tk_bev,))
        return u

    def u_wT():
        tk = None
        w0 = (tk_selb,) + tuple(last_r_ps[0]) + tuple(last_r_ps[1]) + \
             ((Z["tk_bev_prev"],) if Z["tk_bev_prev"] else ()) + \
             ((Z["tk_pe"],) if Z["tk_pe"] else ())
        for off in range(0, TOWN, 512):
            n = min(512, TOWN - off)
            tk = t.op("matmul", PS[2 + off // 1024][0:E, off % 1024: off % 1024 + n],
                      waits=w0 if off == 0 else (),
                      produce=(off + n >= TOWN),
                      lhsT=sel_b[:], rhs=piT[:, off:off + n], start=True, stop=True)
        Z["tk_wT"] = tk
        tke = None
        for i in range(NSEG):
            n = min(1024, TOWN - i * 1024)
            tke = a.op("activation", waits=(Z["tk_wT"],),
                       out=wT_s[:, i * 1024:i * 1024 + n],
                       in_=PS[2 + i][0:E, 0:n], func=AF.Copy)
        Z["tk_wTe"] = tke

    def u_wbc():
        tk = None
        for off in range(0, TOWN, 512):
            n = min(512, TOWN - off)
            tk = t.op("matmul", PS[2 + off // 1024][:, off % 1024: off % 1024 + n],
                      waits=(tk_x16, Z["tk_wTe"]) if off == 0 else (),
                      produce=(off + n >= TOWN),
                      lhsT=exp16_s[:], rhs=wT_s[:, off:off + n], start=True, stop=True)
        Z["tk_wbc"] = tk
        tke = None
        for i in range(NSEG):
            n = min(1024, TOWN - i * 1024)
            tke = a.op("activation", waits=(Z["tk_wbc"],),
                       out=wbc_s[:, i * 1024:i * 1024 + n],
                       in_=PS[2 + i][:, 0:n], func=AF.Copy)
        Z["tk_wbce"] = tke

    def mk_hT(dc0, dc1):
        def u():
            tk = Z["tk_hT"]
            for dc in range(dc0, dc1):
                c = 0
                while c < TOWN:
                    n = min(TOWN - c, 512 - (c % 512), QTR - (c % QTR))
                    tk = t.op("matmul", PS[2 + c // 1024][:, c % 1024: c % 1024 + n],
                              waits=(tk_AT, Z["tk_wbce"])
                                    if (dc == 0 and c == 0) else (),
                              produce=(dc == DC - 1 and c + n >= TOWN),
                              lhsT=AT_s[:, dc, :],
                              rhs=xnT[c // QTR][:, dc, c % QTR: c % QTR + n],
                              start=(dc == 0), stop=(dc == DC - 1))
                    c += n
            Z["tk_hT"] = tk
        return u

    def u_gT():
        tk = None
        for i in range(NSEG):
            n = min(1024, TOWN - i * 1024)
            tk = v.op("tensor_tensor", waits=(Z["tk_hT"], Z["tk_wbce"]),
                      out=gT_s[:, i * 1024:i * 1024 + n],
                      in0=PS[2 + i][:, 0:n],
                      in1=wbc_s[:, i * 1024:i * 1024 + n], op=ALU.mult)
            Z["last_lo"][i].append(tk)
        Z["tk_gT"] = tk
        Z["tk_BTs"] = g.dma(BT_s[:, 0:D], BT_park.ap()[:, :], "g_bts",
                            waits=(Z["tk_BT"], Z["tk_hT"]))

    def mk_lora(o8):
        def u():
            for oc in range(o8 * 8, o8 * 8 + 8):
                lpar = oc % 2
                tk_le = None
                for seg in range(NSEG):
                    pq = PS[2 + seg]
                    tk2 = None
                    for c2 in range(2):
                        tk2 = t.op("matmul", pq[:, c2 * 512:c2 * 512 + 512],
                                   waits=((Z["tk_BTs"], Z["tk_gT"]) +
                                          tuple(Z["last_lo"][seg]))
                                         if c2 == 0 else (),
                                   produce=(c2 == 1),
                                   lhsT=BT_s[:, oc * 128:(oc + 1) * 128],
                                   rhs=gT_s[:, seg * 1024 + c2 * 512:
                                            seg * 1024 + c2 * 512 + 512],
                                   start=True, stop=True)
                    Z["last_lo"][seg][:] = []
                    tk_le = a.op("activation",
                                 waits=(tk2,) + tuple(Z["last_lev"][lpar]),
                                 out=lora_ev[:, lpar, :],
                                 in_=pq[:, 0:1024], func=AF.Copy)
                    Z["last_lo"][seg].append(tk_le)
                    Z["last_lev"][lpar][:] = []
                    tk_lp = a.dma(LORA_park.ap()[oc * 128:(oc + 1) * 128,
                                                 seg * 1024:seg * 1024 + 1024],
                                  lora_ev[:, lpar, :], "a_lp%d" % lpar,
                                  waits=(tk_le,))
                    Z["last_lev"][lpar].append(tk_lp)
        return u

    Zb = {"last_bwt": [[], []], "last_f_ps": [[], []], "last_fev": [[], []],
          "tk_bt": [None, None], "prev_bt": None}

    def mk_base(oc, seg):
        def u():
            par = oc % 2
            if seg == 0:
                tk_bw = s.dma(bw_row[:], bW_b.ap()[oc * 128:(oc + 1) * 128, :],
                              "s_bw%d" % par,
                              waits=tuple(Zb["last_bwt"][par]) + (tk_bW_b,) +
                                    ((Zb["prev_bt"],) if Zb["prev_bt"] else ()))
                tk_bt = s.dma_tr(bWT_t[:, par, :, :],
                                 bw_row.rearrange("p (c f) -> p c f", f=128),
                                 "s_bw%d" % par, waits=(tk_bw,))
                Zb["prev_bt"] = tk_bt
                Zb["tk_bt"][par] = tk_bt
                Zb["last_bwt"][par][:] = []
            q = (2 * oc + seg) % 2
            tkmm = None
            for dc in range(DC):
                for c2 in range(2):
                    c = seg * 1024 + c2 * 512
                    tkmm = t.op("matmul", PS[q][:, c2 * 512:c2 * 512 + 512],
                                waits=((Zb["tk_bt"][par],) +
                                       tuple(Zb["last_f_ps"][q]))
                                      if (dc == 0 and c2 == 0) else (),
                                produce=(dc == DC - 1 and c2 == 1),
                                lhsT=bWT_t[:, par, dc, :],
                                rhs=xnT[c // QTR][:, dc, c % QTR: c % QTR + 512],
                                start=(dc == 0), stop=(dc == DC - 1))
            Zb["last_f_ps"][q][:] = []
            if seg == 1:
                Zb["last_bwt"][par].append(tkmm)
            tk_fe = v.op("tensor_scalar", waits=(tkmm,) + tuple(Zb["last_fev"][q]),
                         out=ffn_ev[:, q, :],
                         in0=PS[q][:, 0:1024],
                         scalar1=bb_cols[:, oc:oc + 1], scalar2=None, op0=ALU.add)
            Zb["last_f_ps"][q].append(tk_fe)
            Zb["last_fev"][q][:] = []
            tk_fp = a.dma(FFN_park.ap()[oc * 128:(oc + 1) * 128,
                                        seg * 1024:seg * 1024 + 1024],
                          ffn_ev[:, q, :], "a_ffn%d" % q, waits=(tk_fe,))
            Zb["last_fev"][q].append(tk_fp)
        return u

    runits = [f for tc in range(NT) for f in (mk_tc(tc), mk_tc_b(tc))] + \
             [mk_bt(e, oc0) for e in range(E) for oc0 in range(0, DC, 8)] + \
             [u_wT, u_wbc, mk_hT(0, DC // 2), mk_hT(DC // 2, DC), u_gT] + \
             [mk_lora(o8) for o8 in range(DC // 8)]
    bunits = [mk_base(oc, seg) for oc in range(DC) for seg in range(NSEG)]
    ri = 0
    for bi, bu in enumerate(bunits):
        bu()
        for _ in range(2 if bi < 8 else 1):
            if ri < len(runits):
                runits[ri]()
                ri += 1
    while ri < len(runits):
        runits[ri]()
        ri += 1

    rec.barrier()

    last_h_ld = [[], []]
    last_ao = [[], []]
    last_pT = [[] for _ in range(8)]
    last_sT = [[], [], [], []]
    last_ot = [[], []]
    last_ivb = []
    last_mg = [[], []]
    last_tl = []
    qh_glob = 0
    pending_tail = []
    park_AO = []
    SLOT = lambda i: PSALL.ap()[:, i * 512:(i + 1) * 512]
    OT = lambda i: PSALL.ap()[:, 2048 + i * 512: 2048 + (i + 1) * 512]
    TL = PSALL.ap()[:, 3072:3584]
    TLA = lambda r: PSALL.ap()[r * 32:r * 32 + 1, 3584:3584 + QH]
    tla_guard = {0: None, 1: None}
    for h in range(H):
        par = h % 2
        sem_h = "s_kv%d" % par
        s.dma(Kh_sb[:, par, :], K_park.ap()[h, :, :], sem_h,
              waits=tuple(last_h_ld[par]))
        s.dma(Qh_sb[:, par, :], Q_park.ap()[h, :, :], sem_h)
        tk_ld = s.dma_tr(Vh_sb[:, par, :, :],
                         VT_park.ap()[h, :, :].rearrange("p (c f) -> p c f", f=128),
                         sem_h)
        last_h_ld[par] = []
        head_toks = []
        tk_pv_last = None
        for qh in range(NQH):
            oi = qh_glob % 2
            ot_cur, ot_nxt = OT(oi), OT(1 - oi)
            qh_glob += 1
            if pending_tail:
                pending_tail.pop(0)()
            npairs = (qh + 1) * QH // 128
            tk_exp = {}
            first = True
            pend = None
            tk_pv_con = None
            tla_st = {"tk": None}

            def emit_pv(p):
                nonlocal first, tk_pv_con
                ptr_, qs_, tk_m_ = p
                cpx = cpv[ptr_]
                for kind in (0, 1):
                    sl_i = ptr_ * 2 + kind
                    koff = kind * TOWN + cpx * 128
                    wts = (tk_m_,)
                    if first:
                        wts = wts + tuple(last_ot[oi]) + (tk_ld,)
                    tk = t.op("matmul", ot_cur[:, qs_:QH],
                              waits=wts,
                              produce=(kind == 1),
                              lhsT=Vh_sb[:, par, koff // 128, :],
                              rhs=pT_sb[:, sl_i, qs_:QH],
                              start=first,
                              stop=(cpx == npairs - 1 and kind == 1))
                    first = False
                tk_pv_con = tk
                for kind in (0, 1):
                    gw = ()
                    if cpx == 0 and kind == 0 and tla_guard[oi] is not None:
                        gw = (tla_guard[oi],)
                    tk_t = t.op("matmul", TLA(oi)[:, qs_:QH], waits=gw,
                                produce=(kind == 1),
                                lhsT=ones_col[:],
                                rhs=pT_sb[:, ptr_ * 2 + kind, qs_:QH],
                                start=(cpx == 0 and kind == 0),
                                stop=(cpx == npairs - 1 and kind == 1))
                tla_st["tk"] = tk_t
                last_pT[ptr_ * 2].append(tk_t)
                last_pT[ptr_ * 2 + 1].append(tk_t)

            cpv = {}
            qsv = {}
            for cp in range(npairs):
                qs = max(0, cp * 128 - qh * QH)
                prng = cp % 2
                ptr = cp % 4
                cpv[ptr] = cp
                qsv[ptr] = qs
                for kind in (0, 1):
                    sl_i = prng * 2 + kind
                    koff = kind * TOWN + cp * 128
                    tk_s = t.op("matmul", SLOT(sl_i)[:, qs:QH],
                                waits=(tk_ld,) + tuple(last_sT[sl_i]),
                                produce=(kind == 1),
                                lhsT=Kh_sb[:, par, koff:koff + 128],
                                rhs=Qh_sb[:, par, qh * QH + qs: (qh + 1) * QH],
                                start=True, stop=True)
                    last_sT[sl_i] = []
                pairview = PSALL.ap()[:, prng * 1024:(prng + 1) * 1024] \
                    .rearrange("p (a q) -> p a q", a=2)[:, :, qs:QH]
                tk_xp = a.op("activation",
                             waits=(tk_s, tk_negC) +
                                   tuple(last_pT[ptr * 2] + last_pT[ptr * 2 + 1]),
                             out=pT_sb[:, ptr * 2:ptr * 2 + 2, qs:QH],
                             in_=pairview,
                             func=AF.Exp, scale=ATT_SCALE, bias=negc_col[:])
                last_pT[ptr * 2] = []
                last_pT[ptr * 2 + 1] = []
                last_sT[prng * 2].append(tk_xp)
                last_sT[prng * 2 + 1].append(tk_xp)
                if cp * 128 >= qh * QH:
                    tk_m = v.op("tensor_tensor", waits=(tk_xp, tk_mp2, tk_mpa),
                                out=pT_sb[:, ptr * 2:ptr * 2 + 2, qs:qs + 128],
                                in0=pT_sb[:, ptr * 2:ptr * 2 + 2, qs:qs + 128],
                                in1=mp2[:], op=ALU.mult)
                else:
                    tk_m = tk_xp
                if cp == 2 and pending_tail:
                    pending_tail.pop(0)()
                if pend is not None:
                    emit_pv(pend)
                pend = (ptr, qs, tk_m)
            emit_pv(pend)
            tk_pv_last = tk_pv_con
            last_ot[oi] = []

            def make_tail(ot_cur=ot_cur, ot_nxt=ot_nxt, oi=oi, qh=qh, par=par,
                          tk_tla=tla_st["tk"], tk_pv_last=tk_pv_last,
                          my_toks=head_toks):
                state = {}

                def part1():
                    tk_ln = a.op("activation", waits=(tk_tla,),
                                 out=invd_s[:, 0:QH],
                                 in_=TLA(oi)[:, 0:QH], func=AF.Ln)
                    tla_guard[oi] = tk_ln
                    state["iv"] = a.op("activation", out=invd_s[:, 0:QH],
                                       in_=invd_s[:, 0:QH], func=AF.Exp, scale=-1.0)

                def part2():
                    nonlocal last_ivb, last_tl
                    tk_bc = t.op("matmul", TL[:, 0:QH],
                                 waits=(state["iv"],) + tuple(last_tl),
                                 produce=True,
                                 lhsT=ones_r128[:], rhs=invd_s[:, 0:QH],
                                 start=True, stop=True)
                    tk_cp = v.op("tensor_copy", waits=(tk_bc,) + tuple(last_ivb),
                                 out=ivb_s[:, 0:QH], in_=TL[:, 0:QH])
                    last_tl = [tk_cp]
                    tk_no = v.op("tensor_tensor",
                                 waits=(tk_pv_last, tk_cp) + tuple(last_ao[par]),
                                 out=ao_sb[:, par, qh * QH:(qh + 1) * QH],
                                 in0=ot_cur[:, 0:QH], in1=ivb_s[:, 0:QH],
                                 op=ALU.mult)
                    last_ivb = [tk_no]
                    last_ao[par] = []
                    last_ot[oi].append(tk_no)
                    my_toks.append(tk_no)
                return [part1, part2]
            pending_tail.extend(make_tail())
        while pending_tail:
            pending_tail.pop(0)()
        tk_pk = a.dma(AO_park.ap()[h, :, :], ao_sb[:, par, :],
                      "a_ao%d" % par, waits=tuple(head_toks))
        last_ao[par] = [tk_pk]
        last_h_ld[par] = [tk_pv_last, tk_pk] + \
                         [tok for l in last_sT for tok in l]
        park_AO.append(tk_pk)
        if h < DC:
            ml = h % 2
            tk_f1 = s.dma(ffn_ld[:, ml, :], FFN_park.ap()[h * 128:(h + 1) * 128, :],
                          "s_mg%d" % ml, waits=tuple(last_mg[ml]))
            tk_f2 = s.dma(lora_ld[:, ml, :], LORA_park.ap()[h * 128:(h + 1) * 128, :],
                          "s_mg%d" % ml)
            tk_ma = v.op("tensor_tensor", waits=(tk_f2,),
                         out=ffn_ld[:, ml, :], in0=ffn_ld[:, ml, :],
                         in1=lora_ld[:, ml, :], op=ALU.add)
            tk_mb = a.dma(FFN_park.ap()[h * 128:(h + 1) * 128, :],
                          ffn_ld[:, ml, :], "a_mg%d" % ml, waits=(tk_ma,))
            last_mg[ml] = [tk_ma, tk_mb]

    rec.barrier()

    last_at = [[], []]
    last_x4 = [[], []]
    last_f4 = [[], []]
    last_st = [[], []]
    last_p4ps = [[], []]
    prev_oh_mm = None
    n_store = 0
    for oh in range(NOH):
        tk_wo = None
        for dc in range(DC):
            tk_wo = g.dma(woh_sb[:, dc, :],
                          wo_in[dc * 128:(dc + 1) * 128, oh * OHALF:(oh + 1) * OHALF],
                          "g_wo",
                          waits=(prev_oh_mm,) if (dc == 0 and prev_oh_mm) else ())
        for tt in range(NT):
            par = tt % 2
            pspair = [PS[0], PS[1]] if par == 0 else [PS[2], PS[3]]
            sem_t = "s_p4%d" % par
            nspl = max(1, H // 8)
            for sp in range(nspl):
                h0, h1 = sp * (H // nspl), (sp + 1) * (H // nspl)
                s.dma(atT_sb[:, par, h0:h1, :],
                      AO_park.ap()[h0:h1, :, tt * 128:(tt + 1) * 128]
                           .rearrange("h p f -> p h f"), sem_t,
                      waits=tuple(last_at[par]) + tuple(last_f4[par]) +
                            tuple(last_x4[par]))
            last_at[par] = []
            s.dma_tr(ffn_p4[:, par, :],
                     FFN_park.ap()[oh * OHALF:(oh + 1) * OHALF,
                                   tt * 128:(tt + 1) * 128], sem_t)
            last_f4[par] = []
            tk_x4 = s.dma(x_p4[:, par, :],
                          x_in[tt * 128:(tt + 1) * 128, oh * OHALF:(oh + 1) * OHALF],
                          sem_t)
            last_x4[par] = []
            tk_at = tk_f4 = tk_x4
            tkmm = None
            for dc in range(DC):
                for off in range(0, OHALF, 512):
                    n = min(512, OHALF - off)
                    tkmm = t.op("matmul",
                                pspair[off // 1024][:, off % 1024: off % 1024 + n],
                                waits=((tk_at, tk_wo) + tuple(last_p4ps[par]))
                                      if (dc == 0 and off == 0) else (),
                                produce=(dc == DC - 1 and off + n >= OHALF),
                                lhsT=atT_sb[:, par, dc, :],
                                rhs=woh_sb[:, dc, off:off + n],
                                start=(dc == 0), stop=(dc == DC - 1))
            last_p4ps[par] = []
            last_at[par].append(tkmm)
            prev_oh_mm = tkmm
            tk_a2 = None
            for i in range((OHALF + 1023) // 1024):
                n = min(1024, OHALF - i * 1024)
                sl = slice(i * 1024, i * 1024 + n)
                tk_a1 = v.op("tensor_tensor",
                             waits=(tkmm, tk_f4) + tuple(last_st[par]),
                             out=out_p4[:, par, sl], in0=pspair[i][:, 0:n],
                             in1=ffn_p4[:, par, sl], op=ALU.add)
                last_p4ps[par].append(tk_a1)
                tk_a2 = v.op("tensor_tensor", waits=(tk_x4,),
                             out=out_p4[:, par, sl], in0=out_p4[:, par, sl],
                             in1=x_p4[:, par, sl], op=ALU.add)
            last_st[par] = []
            last_x4[par].append(tk_a2)
            last_f4[par].append(tk_a2)
            tk_st = a.dma(out_d[tt * 128:(tt + 1) * 128,
                                oh * OHALF:(oh + 1) * OHALF],
                          out_p4[:, par, :], "st%d" % par, waits=(tk_a2,))
            last_st[par] = [tk_st]
            n_store += 1
    s.op("nop", waits=tuple(rec.queue_totals()))

    rec.replay()
    return nc



_PROG_CACHE = {}

def _get_prog():
    if "full" not in _PROG_CACHE:
        _PROG_CACHE["full"] = build_program(CFG_FULL)
    return _PROG_CACHE["full"]


def make_in_maps(cfg, x, wq, wk, wv, wo, base_W, base_b, router_W, router_b,
                 A, B, expert_map, n_cores=8):
    D, S = cfg["D"], cfg["S"]
    NR, E, R = cfg["NR"], cfg["E"], cfg["R"]
    shared = {
        "wq": np.ascontiguousarray(wq, np.float32),
        "wk": np.ascontiguousarray(wk, np.float32),
        "wv": np.ascontiguousarray(wv, np.float32),
        "wo": np.ascontiguousarray(wo, np.float32),
        "base_W": np.ascontiguousarray(base_W, np.float32),
        "base_b": np.ascontiguousarray(base_b, np.float32).reshape(D // 128, 128),
        "router_W": np.ascontiguousarray(router_W, np.float32),
        "router_b": np.ascontiguousarray(router_b, np.float32).reshape(1, NR),
        "A_flat": np.ascontiguousarray(A, np.float32).reshape(E * R, D),
        "B_w": np.ascontiguousarray(B, np.float32).reshape(E * D, R),
        "expert_map": np.ascontiguousarray(expert_map, np.int32).reshape(NR, 1),
    }
    in_maps, orders = [], []
    for c in range(n_cores):
        bc, pc = c // 2, c % 2
        order = [2 * l + pc for l in range(S // 256)] + \
                [2 * l + (1 - pc) for l in range(S // 256)]
        orders.append(order)
        xb = np.ascontiguousarray(
            x[bc].reshape(S // 128, 128, D)[order].reshape(S, D), np.float32)
        pos = (np.asarray(order)[:, None] * 128 +
               np.arange(128)[None, :]).reshape(1, S).astype(np.float32)
        mpart = np.full((128, 128), 0.0 if pc == 0 else 1.0,
                        dtype=ml_dtypes.bfloat16)
        m = dict(shared)
        m["x_perm"] = xb
        m["pos_perm"] = pos
        m["m_part"] = mpart
        in_maps.append(m)
    return in_maps, orders


def assemble_out(cfg, results, orders, Bn):
    D, S = cfg["D"], cfg["S"]
    TOWN = S // 2
    out = np.empty((Bn, S, D), dtype=np.float32)
    for c, r in enumerate(results):
        bc = c // 2
        shard = np.asarray(r["out_own"]).reshape(TOWN // 128, 128, D)
        for l in range(TOWN // 128):
            gt = orders[c][l]
            out[bc, gt * 128:(gt + 1) * 128, :] = shard[l]
    return out


def kernel(L0_attn_hidden_states, L0_attn_wq, L0_attn_wk, L0_attn_wv, L0_attn_wo,
           L0_attn_kq1, L0_attn_kq2, L0_attn_kk1, L0_attn_kk2, L0_attn_scalingD,
           L0_attn_interaction_k, L0_attn_scalingD2, L0_ffn_base_W, L0_ffn_base_b,
           L0_ffn_router_W, L0_ffn_router_b, L0_ffn_A, L0_ffn_B, L0_ffn_expert_map):
    cfg = CFG_FULL
    x = np.ascontiguousarray(L0_attn_hidden_states, dtype=np.float32)
    nc = _get_prog()
    in_maps, orders = make_in_maps(
        cfg, x, L0_attn_wq, L0_attn_wk, L0_attn_wv, L0_attn_wo,
        L0_ffn_base_W, L0_ffn_base_b, L0_ffn_router_W, L0_ffn_router_b,
        L0_ffn_A, L0_ffn_B, L0_ffn_expert_map)
    res = run_bass_kernel_spmd(nc, in_maps, core_ids=list(range(8)))
    return assemble_out(cfg, res.results, orders, x.shape[0])
